# version 01
# baseline (speedup 1.0000x reference)
import numpy as np
import concourse.bass as bass


class _Done(Exception):
    pass
import concourse.mybir as mybir

F32 = mybir.dt.float32
BF16 = mybir.dt.bfloat16
AX = mybir.AxisListType.X
ALU = mybir.AluOpType
ACT = mybir.ActivationFunctionType

D, H, E, S, G = 1024, 2048, 8, 4096, 2048
NTG = G // 128
NTS = S // 128
CC = D // 128
HH = H // 128
NK = S // 512
NQ = G // 128

C_INV_D, C_NEG_INV_D, C_EPS, C_HALF, C_NEGHALF, C_NEGBIG, C_ONE, C_NEGONE = range(8)

ENGINES = ("tensor", "vector", "scalar", "gpsimd", "sync")


class Sem:
    __slots__ = ("h", "name", "n")
    def __init__(self, h, name):
        self.h, self.name, self.n = h, name, 0


class Plan:
    def __init__(self):
        self.q = {e: [] for e in ENGINES}
        self.meta = {e: [] for e in ENGINES}

    def op(self, eng, fn, incs=()):
        self.q[eng].append(fn)
        self.meta[eng].append(("i", list(incs)))
        for s, k in incs:
            s.n += k

    def wait(self, eng, sem, val, tag=""):
        assert val <= sem.n, f"[{eng}] wait {sem.name}>={val} but only {sem.n} planned ({tag})"
        self.q[eng].append(lambda x, s=sem, v=val: x.wait_ge(s.h, v))
        self.meta[eng].append(("w", sem, val, tag))

    def verify(self):
        pos = {e: 0 for e in ENGINES}
        cur = {}
        progress = True
        while progress:
            progress = False
            for e in ENGINES:
                while pos[e] < len(self.meta[e]):
                    m = self.meta[e][pos[e]]
                    if m[0] == "w":
                        if cur.get(m[1].name, 0) < m[2]:
                            break
                    else:
                        for s, k in m[1]:
                            cur[s.name] = cur.get(s.name, 0) + k
                    pos[e] += 1
                    progress = True
        stuck = {e: (pos[e], len(self.meta[e])) for e in ENGINES if pos[e] < len(self.meta[e])}
        if stuck:
            det = {}
            for e in stuck:
                m = self.meta[e][pos[e]]
                det[e] = f"at {pos[e]}/{len(self.meta[e])}: waiting {m[1].name}>={m[2]} (cur {cur.get(m[1].name,0)}) tag={m[3]}"
            raise RuntimeError(f"DEADLOCK: {det}")


def build(upto="full", n_groups=2, n_experts=8, smoke=None):
    nc = bass.Bass("TRN2", target_bir_lowering=False, debug=False, detect_race_conditions=False)

    x_in = nc.dram_tensor("x", [S, D], F32, kind="ExternalInput").ap()
    w1_in = nc.dram_tensor("w1", [E, D, H], F32, kind="ExternalInput").ap()
    w2_in = nc.dram_tensor("w2", [E, H, D], F32, kind="ExternalInput").ap()
    wg_in = nc.dram_tensor("wg", [128, CC * E], F32, kind="ExternalInput").ap()
    wa_in = nc.dram_tensor("wa", [D, D], F32, kind="ExternalInput").ap()
    ba_in = nc.dram_tensor("ba", [128, CC], F32, kind="ExternalInput").ap()
    cst_in = nc.dram_tensor("cst", [128, 8], F32, kind="ExternalInput").ap()
    idn_in = nc.dram_tensor("idn", [128, 128], F32, kind="ExternalInput").ap()
    band_in = nc.dram_tensor("band", [128, 5 * 128], F32, kind="ExternalInput").ap()

    out_rows = S if upto == "ffn1" else G
    y_out = nc.dram_tensor("y", [out_rows, D], F32, kind="ExternalOutput").ap()
    x1buf = nc.dram_tensor("x1buf", [S, D], F32).ap()
    x2buf = nc.dram_tensor("x2buf", [G, D], F32).ap()

    x_v = x_in.rearrange("(tt p) d -> tt p d", p=128)
    x1_v = x1buf.rearrange("(tt p) d -> tt p d", p=128)
    x2_v = x2buf.rearrange("(tt p) d -> tt p d", p=128)
    y_v = y_out.rearrange("(tt p) d -> tt p d", p=128)

    P = Plan()

    from contextlib import ExitStack
    with ExitStack() as stack:
        def sb(name, shape, dt):
            return stack.enter_context(nc.sbuf_tensor(name, shape, dt))
        def pt(name, shape, dt):
            return stack.enter_context(nc.psum_tensor(name, shape, dt))
        big1 = sb("big1", [128, 32768], BF16)
        big2 = sb("big2", [128, 32768], BF16)
        bigw = sb("bigw", [128, 32768], BF16)
        sqt = sb("sqt", [128, 1024], F32)
        xnt_b = sb("xnt2", [128, 2 * 1024], BF16)
        t1b = sqt.ap().bitcast(BF16)[0:128, 0:1024]
        stat = sb("stat", [128, 64], F32)
        bab = sb("bab", [128, 128], F32)
        comb = sb("comb", [128, NTG * E], F32)
        gsc = sb("gsc", [128, 16], F32)
        gm = sb("gm", [128, 8], F32)
        cstS = sb("cstS", [128, 8], F32)
        idnS = sb("idnS", [128, 128], BF16)
        bandS = sb("bandS", [128, 5 * 128], F32)
        wgF = sb("wgF", [128, CC * E], F32)
        idnF = sb("idnF", [128, 128], F32)
        xtf = sb("xtf", [128, 512], F32)
        baS = sb("baS", [128, CC], F32)
        ps0 = pt("ps0", [128, 512], F32); ps1 = pt("ps1", [128, 512], F32)
        ps2 = pt("ps2", [128, 512], F32); ps3 = pt("ps3", [128, 512], F32)
        ps4 = pt("ps4", [128, 512], F32); ps5 = pt("ps5", [128, 512], F32)
        ps6 = pt("ps6", [128, 512], F32)
        pstr = pt("pstr", [128, 512], BF16)
        block = stack.enter_context(nc.Block())

        psf = [ps0, ps1, ps2, ps3, ps4, ps5, ps6]

        def emit_all():
            @block.tensor
            def _(t):
                for fn in P.q["tensor"]: fn(t)
            @block.vector
            def _(v):
                for fn in P.q["vector"]: fn(v)
            @block.scalar
            def _(s):
                for fn in P.q["scalar"]: fn(s)
            @block.gpsimd
            def _(g):
                for fn in P.q["gpsimd"]: fn(g)
            @block.sync
            def _(s):
                for fn in P.q["sync"]: fn(s)

        sems = {}
        sem_ctxs = []
        def sem(name):
            cm = nc.semaphore(name)
            h = cm.__enter__()
            sem_ctxs.append(cm)
            s = Sem(h, name)
            sems[name] = s
            return s

        s_const = sem("sconst"); s_const2 = sem("sconst2")
        s_ldx = sem("sldx")
        s_w1 = sem("sw1"); s_w2 = sem("sw2"); s_wa = sem("swa")
        s_sq = sem("ssq")
        s_stats = sem("sstats")
        s_xn = sem("sxn")
        s_tr = sem("str")
        s_xnt = sem("sxnt")
        s_gmm = sem("sgmm"); s_gps = sem("sgps"); s_gdd = sem("sgdd"); s_gx = sem("sgx"); s_gcb = sem("sgcb")
        s_xnf = sem("sxnf"); s_gtr = sem("sgtr"); s_gev = sem("sgev"); s_vb = sem("svb")
        s_hmm = sem("shmm"); s_t1 = sem("st1"); s_u = sem("su"); s_hp = sem("shp")
        s_eomm = sem("seomm"); s_eoev = sem("seoev")
        s_spill = sem("sspill")
        s_xs = sem("sxs")
        s_fmm = sem("sfmm"); s_fev = sem("sfev")
        s_smm = sem("ssmm"); s_sev = sem("ssev"); s_mx = sem("smx"); s_exp = sem("sexp")
        s_omm = sem("somm"); s_oev = sem("soev"); s_atr = sem("satr")
        s_x1o = sem("sx1o")

        def t_op(fn, incs=()): P.op("tensor", fn, incs)
        def vg_op(fn, incs=()):
            P.op("vector", lambda v: (v.nop(), v.nop(), v.nop()), ())
            P.op("vector", fn, incs)
        def v_op(fn, incs=()): P.op("vector", fn, incs)
        def sc_op(fn, incs=()): P.op("scalar", fn, incs)
        def g_op(fn, incs=()): P.op("gpsimd", fn, incs)
        def sy_op(fn, incs=()): P.op("sync", fn, incs)

        g_op(lambda g: g.dma_start(out=idnS[:], in_=idn_in).then_inc(s_const.h, 16), [(s_const, 16)])
        g_op(lambda g: g.dma_start(out=wgF[:], in_=wg_in).then_inc(s_const.h, 16), [(s_const, 16)])
        g_op(lambda g: g.dma_start(out=idnF[:], in_=idn_in).then_inc(s_const.h, 16), [(s_const, 16)])
        g_op(lambda g: g.dma_start(out=cstS[:], in_=cst_in).then_inc(s_const.h, 16), [(s_const, 16)])
        sy_op(lambda s: s.dma_start(out=bandS[:], in_=band_in).then_inc(s_const2.h, 16), [(s_const2, 16)])
        g_op(lambda g: g.dma_start(out=baS[:], in_=ba_in).then_inc(s_const.h, 16), [(s_const, 16)])
        CONST_T = s_const.n
        P.wait("vector", s_const2, s_const2.n, "const2 v")
        P.wait("vector", s_const, CONST_T, "const v")
        P.wait("scalar", s_const, CONST_T, "const s")
        P.wait("tensor", s_const, CONST_T, "const t")
        sc_op(lambda s: s.activation(out=gm[:, 0:1], in_=cstS[:, 0:1], func=ACT.Exp))

        def c(i):
            return cstS[:, i:i + 1]

        xg = big1.ap().bitcast(F32)
        xnT = big2[:, 0:CC * G]
        hp = [big2[:, 16384 + i * 8192: 16384 + (i + 1) * 8192] for i in range(2)]
        w1S = bigw[:, 0:CC * H]
        w2S = bigw[:, CC * H: CC * H + HH * D]
        fT = big1.ap()
        f_tok = big2.ap()
        aw = bigw.ap()
        waS = aw[0:128, 0:CC * D]
        xs_f = aw[0:128, CC * D: CC * D + 4096].bitcast(F32)
        xn2_b = aw[0:128, CC * D + 4096: CC * D + 4096 + 2048]
        xnT2 = aw[0:128, CC * D + 6144: CC * D + 6144 + 2 * 4096]
        scoresS = aw[0:128, 0:8192].bitcast(F32)
        a_b = [aw[0:128, 8192 + i * 4096: 8192 + (i + 1) * 4096] for i in range(2)]
        aT_b = [aw[0:128, 16384 + i * 4096: 16384 + (i + 1) * 4096] for i in range(2)]
        x1o_f = aw[0:128, 24576:24576 + 4096].bitcast(F32)
        x2st_f = aw[0:128, 28672:28672 + 4096].bitcast(F32)

        def ln_stats(ntile, get_tile):
            base_sq = s_sq.n
            sums = bab[:, 96:96 + ntile] if False else stat[:, 0:ntile]
            ssq = bab[:, 96:96 + ntile]
            for i in range(ntile):
                tile, waits = get_tile(i)
                for sx, vv in waits:
                    P.wait("scalar", sx, vv, "lnst")
                P.wait("scalar", s_sq, s_sq.n, "sq waw")
                sc_op(lambda s, t=tile, o=sums[:, i:i + 1]: s.activation(out=sqt[:], in_=t, func=ACT.Copy, accum_out=o).then_inc(s_sq.h, 1), [(s_sq, 1)])
                P.wait("scalar", s_sq, s_sq.n, "sq waw2")
                sc_op(lambda s, t=tile, o=ssq[:, i:i + 1]: s.activation(out=sqt[:], in_=t, func=ACT.Square, accum_out=o).then_inc(s_sq.h, 1), [(s_sq, 1)])
            nm = stat[:, 0:ntile]
            rs = stat[:, 32:32 + ntile]
            a_ = bab[:, 0:ntile]; z_ = bab[:, 32:32 + ntile]; t_ = bab[:, 64:64 + ntile]
            P.wait("vector", s_sq, base_sq + 2 * ntile, "params")
            vg_op(lambda v: v.tensor_scalar_mul(out=t_, in0=sums, scalar1=c(C_INV_D)))
            vg_op(lambda v: v.tensor_scalar_mul(out=a_, in0=ssq, scalar1=c(C_INV_D)))
            vg_op(lambda v: v.tensor_tensor(out=z_, in0=t_, in1=t_, op=ALU.mult))
            vg_op(lambda v: v.tensor_tensor(out=a_, in0=a_, in1=z_, op=ALU.subtract))
            vg_op(lambda v: v.tensor_scalar_add(out=a_, in0=a_, scalar1=c(C_EPS)))
            vg_op(lambda v: v.tensor_scalar_mul(out=nm, in0=t_, scalar1=c(C_NEGONE)))
            vg_op(lambda v: v.tensor_scalar(out=z_, in0=a_, scalar1=c(C_HALF), scalar2=c(C_HALF), op0=ALU.mult, op1=ALU.add))
            for _ in range(13):
                vg_op(lambda v: v.reciprocal(out=t_, in_=z_))
                vg_op(lambda v: v.tensor_tensor(out=t_, in0=a_, in1=t_, op=ALU.mult))
                vg_op(lambda v: v.tensor_tensor(out=t_, in0=t_, in1=z_, op=ALU.add))
                vg_op(lambda v: v.tensor_scalar_mul(out=z_, in0=t_, scalar1=c(C_HALF)))
            vg_op(lambda v: v.reciprocal(out=rs, in_=z_).then_inc(s_stats.h, 1), [(s_stats, 1)])
            return nm, rs, s_stats.n

        def xn_transpose(i, xb, dst_cc_stride, dst_col, src_tile):
            for half in range(2):
                P.wait("tensor", s_xn, s_xn.n, "trxn")
                P.wait("tensor", s_xnt, s_tr.n, "pstrfree")
                for k in range(4):
                    cc2 = half * 4 + k
                    last = (k == 3)
                    t_op(lambda t, o=pstr[:, k * 128:(k + 1) * 128], i_=xb[:, cc2 * 128:(cc2 + 1) * 128], li=last:
                         (t.transpose(o, i_, idnS[:]).then_inc(s_tr.h, 1) if li
                          else t.transpose(o, i_, idnS[:])),
                         [(s_tr, 1)] if last else ())
                P.wait("vector", s_tr, s_tr.n, "evtr")
                for k in range(4):
                    cc2 = half * 4 + k
                    last = (k == 3)
                    v_op(lambda v, o=src_tile[:, cc2 * dst_cc_stride + dst_col: cc2 * dst_cc_stride + dst_col + 128],
                         i_=pstr[:, k * 128:(k + 1) * 128], li=last:
                         (v.tensor_copy(out=o, in_=i_).then_inc(s_xnt.h, 1) if li
                          else v.tensor_copy(out=o, in_=i_)),
                         [(s_xnt, 1)] if last else ())

        def emit_gating_tail(i):
                sc0 = gsc[:, 0:E]; s2 = gsc[:, E:2 * E]
                m1 = gm[:, 0:1]; m2 = gm[:, 1:2]; p1 = gm[:, 2:3]; p2 = gm[:, 3:4]; dd_ = gm[:, 4:5]
                P.wait("vector", s_gmm, s_gmm.n, "gev")
                vg_op(lambda v: v.tensor_copy(out=sc0, in_=ps6[:, 0:E]).then_inc(s_gps.h, 1), [(s_gps, 1)])
                vg_op(lambda v: v.tensor_reduce(out=m1, in_=sc0, axis=AX, op=ALU.max))
                vg_op(lambda v: v.tensor_scalar(out=s2, in0=sc0, scalar1=m1, scalar2=None, op0=ALU.is_ge))
                vg_op(lambda v: v.scalar_tensor_tensor(out=s2, in0=s2, scalar=c(C_NEGBIG), in1=sc0, op0=ALU.mult, op1=ALU.add))
                vg_op(lambda v: v.tensor_reduce(out=m2, in_=s2, axis=AX, op=ALU.max))
                vg_op(lambda v: v.tensor_tensor(out=dd_, in0=m2, in1=m1, op=ALU.subtract).then_inc(s_gdd.h, 1), [(s_gdd, 1)])
                P.wait("scalar", s_gdd, s_gdd.n, "gexp")
                sc_op(lambda s: s.activation(out=p2, in_=dd_, func=ACT.Exp).then_inc(s_gx.h, 1), [(s_gx, 1)])
                P.wait("vector", s_gx, s_gx.n, "gp1")
                vg_op(lambda v: v.tensor_scalar_add(out=p2, in0=p2, scalar1=c(C_ONE)))
                vg_op(lambda v: v.reciprocal(out=p1, in_=p2))
                vg_op(lambda v: v.tensor_scalar(out=p2, in0=p1, scalar1=c(C_NEGHALF), scalar2=c(C_HALF), op0=ALU.mult, op1=ALU.add))
                vg_op(lambda v: v.tensor_scalar_mul(out=p1, in0=p1, scalar1=c(C_HALF)))
                cmb = comb[:, i * E:(i + 1) * E]
                vg_op(lambda v, cm=cmb: v.tensor_scalar(out=cm, in0=s2, scalar1=m2, scalar2=None, op0=ALU.is_ge))
                vg_op(lambda v, cm=cmb: v.tensor_scalar_mul(out=cm, in0=cm, scalar1=p2))
                vg_op(lambda v: v.tensor_scalar(out=s2, in0=sc0, scalar1=m1, scalar2=None, op0=ALU.is_ge))
                vg_op(lambda v, cm=cmb: v.scalar_tensor_tensor(out=cm, in0=s2, scalar=p1, in1=cm, op0=ALU.mult, op1=ALU.add).then_inc(s_gcb.h, 1),
                     [(s_gcb, 1)])

        def emit_xn_and_gating(i, tile_f32, nm, rs, st_t, dst, dst_stride, dst_col, gate):
            P.wait("vector", s_stats, st_t, "xnparams")
            if gate and s_gtr.n >= 2:
                P.wait("vector", s_gtr, s_gtr.n - 1, "sqt free")
            v_op(lambda v, t=tile_f32, a=nm[:, i:i + 1], r=rs[:, i:i + 1]:
                 v.tensor_scalar(out=sqt[:], in0=t, scalar1=a, scalar2=r, op0=ALU.add, op1=ALU.mult).then_inc(s_xnf.h, 1),
                 [(s_xnf, 1)])
            P.op("vector", lambda v: (v.nop(), v.nop(), v.nop()), ())
            xb = xnt_b[:, (i % 2) * 1024:(i % 2 + 1) * 1024]
            if i >= 2:
                P.wait("vector", s_tr, s_tr.n - 2, "xnbuf")
            v_op(lambda v, o=xb: v.tensor_copy(out=o, in_=sqt[:]).then_inc(s_xn.h, 1), [(s_xn, 1)])
            xn_transpose(i, xb, dst_stride, dst_col, dst)
            if not gate:
                return
            for half in range(2):
                P.wait("tensor", s_xnf, s_xnf.n, "gtr src")
                if s_gtr.n >= 1:
                    P.wait("tensor", s_gev, s_gtr.n, "ps5 free")
                for k in range(4):
                    cc2 = half * 4 + k
                    last = (k == 3)
                    t_op(lambda t, o=ps5[:, k * 128:(k + 1) * 128], i_=sqt[:, cc2 * 128:(cc2 + 1) * 128], li=last:
                         (t.transpose(o, i_, idnF[:]).then_inc(s_gtr.h, 1) if li
                          else t.transpose(o, i_, idnF[:])),
                         [(s_gtr, 1)] if last else ())
                P.wait("vector", s_gtr, s_gtr.n, "gev")
                for k in range(4):
                    cc2 = half * 4 + k
                    last = (k == 3)
                    v_op(lambda v, o=xtf[:, (cc2 % 4) * 128:(cc2 % 4 + 1) * 128], i_=ps5[:, k * 128:(k + 1) * 128], li=last:
                         (v.tensor_copy(out=o, in_=i_).then_inc(s_gev.h, 1) if li
                          else v.tensor_copy(out=o, in_=i_)),
                         [(s_gev, 1)] if last else ())
                P.wait("tensor", s_gev, s_gev.n, "gmm src")
                if half == 0 and s_gmm.n >= 1:
                    P.wait("tensor", s_gps, s_gmm.n, "gpsfree")
                for k in range(4):
                    cc2 = half * 4 + k
                    last = (half == 1 and k == 3)
                    t_op(lambda t, l=xtf[:, (cc2 % 4) * 128:(cc2 % 4 + 1) * 128],
                         r=wgF[:, cc2 * E:(cc2 + 1) * E], st=(cc2 == 0), sp=last, li=last:
                         (t.matmul(ps6[:, 0:E], l, r, start=st, stop=sp).then_inc(s_gmm.h, 1) if li
                          else t.matmul(ps6[:, 0:E], l, r, start=st, stop=sp)),
                         [(s_gmm, 1)] if last else ())
            emit_gating_tail(i)

        GTC = [0]

        def ffn_pass(pass_idx, src_v, dst_v, t0, first_weights, prefetch_next_pass):
            P.wait("tensor", s_sev, s_sev.n, "ffn entry sev")
            P.wait("tensor", s_oev, s_oev.n, "ffn entry oev")
            P.wait("sync", s_spill, s_spill.n, "xgfree")
            sy_op(lambda s: s.dma_start(out=xg.rearrange("p (tt d) -> p tt d", d=D),
                                        in_=src_v[t0:t0 + NTG].rearrange("tt p d -> p tt d")).then_inc(s_ldx.h, 16), [(s_ldx, 16)])
            ldx_t = s_ldx.n

            if first_weights:
                P.wait("gpsimd", s_spill, s_spill.n, "w after spills")
                g_op(lambda g, wv=w1_in[0].rearrange("(cc p) h -> p cc h", p=128):
                     g.dma_start(out=w1S.rearrange("p (cc h) -> p cc h", cc=CC), in_=wv).then_inc(s_w1.h, 16), [(s_w1, 16)])
                g_op(lambda g, wv=w2_in[0].rearrange("(hh p) d -> p hh d", p=128):
                     g.dma_start(out=w2S.rearrange("p (hh d) -> p hh d", hh=HH), in_=wv).then_inc(s_w2.h, 16), [(s_w2, 16)])

            def get_tile(i):
                return xg[:, i * D:(i + 1) * D], [(s_ldx, ldx_t)]
            if smoke == "copy":
                P.wait("sync", s_ldx, ldx_t, "smoke copy")
                sy_op(lambda s: s.dma_start(out=dst_v[t0:t0 + NTG].rearrange("tt p d -> p tt d"),
                                            in_=xg.rearrange("p (tt d) -> p tt d", d=D)).then_inc(s_spill.h, 16),
                      [(s_spill, 16)])
                return
            nm, rs, st_t = ln_stats(NTG, get_tile)
            if smoke == "stats":
                P.wait("sync", s_stats, st_t, "smoke stats")
                sy_op(lambda s: s.dma_start(out=dst_v[t0:t0 + NTG].rearrange("tt p d -> p tt d"),
                                            in_=xg.rearrange("p (tt d) -> p tt d", d=D)).then_inc(s_spill.h, 16),
                      [(s_spill, 16)])
                return

            for i in range(NTG):
                emit_xn_and_gating(i, xg[:, i * D:(i + 1) * D], nm, rs, st_t, xnT, G, i * 128,
                                   gate=(smoke != "xn"))
            xnt_t = s_xnt.n
            if smoke == "xn":
                P.wait("sync", s_xnt, xnt_t, "smoke xn")
                sy_op(lambda s: s.dma_start(out=dst_v[t0:t0 + NTG].rearrange("tt p d -> p tt d"),
                                            in_=xg.rearrange("p (tt d) -> p tt d", d=D)).then_inc(s_spill.h, 16),
                      [(s_spill, 16)])
                return
            gcb_t = s_gcb.n
            if smoke == "gate":
                P.wait("sync", s_gcb, gcb_t, "smoke gate")
                sy_op(lambda s: s.dma_start(out=dst_v[t0:t0 + NTG].rearrange("tt p d -> p tt d"),
                                            in_=xg.rearrange("p (tt d) -> p tt d", d=D)).then_inc(s_spill.h, 16),
                      [(s_spill, 16)])
                return

            for e in range(n_experts):
                w1_t = 16 * (pass_idx * E + e + 1)
                w2_t = 16 * (pass_idx * E + e + 1)
                for tc in range(4):
                    gtc = GTC[0]; GTC[0] += 1
                    P.wait("tensor", s_w1, w1_t, "w1")
                    P.wait("tensor", s_xnt, xnt_t, "h xnt")
                    first_stt = True
                    for hf in range(HH):
                        bank = psf[hf % 4]
                        if s_hmm.n >= 4:
                            P.wait("tensor", s_hp, s_hmm.n - 3, "hbank")
                        for cc2 in range(CC):
                            last = (cc2 == CC - 1)
                            t_op(lambda t, b=bank, l=w1S[:, cc2 * H + hf * 128: cc2 * H + (hf + 1) * 128],
                                 r=xnT[:, cc2 * G + tc * 512: cc2 * G + (tc + 1) * 512],
                                 st=(cc2 == 0), sp=last, li=last:
                                 (t.matmul(b[:], l, r, start=st, stop=sp).then_inc(s_hmm.h, 1) if li
                                  else t.matmul(b[:], l, r, start=st, stop=sp)),
                                 [(s_hmm, 1)] if last else ())
                        P.wait("scalar", s_hmm, s_hmm.n, "tanh")
                        if s_t1.n >= 2:
                            P.wait("scalar", s_hp, s_t1.n - 1, "t1reuse")
                        t1 = t1b[:, (hf % 2) * 512:(hf % 2 + 1) * 512]
                        sc_op(lambda s, o=t1, b=bank: s.activation(out=o, in_=b[:], func=ACT.Tanh, scale=0.5).then_inc(s_t1.h, 1),
                              [(s_t1, 1)])
                        P.wait("vector", s_t1, s_t1.n, "u")
                        uu = t1
                        v_op(lambda v, o=uu: v.tensor_scalar_add(out=o, in0=o, scalar1=c(C_ONE)).then_inc(s_u.h, 1),
                             [(s_u, 1)])
                        P.op("vector", lambda v: (v.nop(), v.nop(), v.nop()), ())
                        if first_stt and gtc >= 2:
                            P.wait("vector", s_eomm, 8 * (gtc - 1), "hpfree")
                        first_stt = False
                        hpd = hp[tc % 2][:, hf * 512:(hf + 1) * 512]
                        v_op(lambda v, o=hpd, b=bank, u_=uu:
                             v.scalar_tensor_tensor(out=o, in0=b[:], scalar=c(C_HALF), in1=u_, op0=ALU.mult, op1=ALU.mult).then_inc(s_hp.h, 1),
                             [(s_hp, 1)])
                    if tc == 3:
                        if e < E - 1 or prefetch_next_pass:
                            ep = (e + 1) % E
                            P.wait("gpsimd", s_hmm, s_hmm.n, "w1pre")
                            w1v = w1_in[ep].rearrange("(cc p) h -> p cc h", p=128)
                            g_op(lambda g, wv=w1v: g.dma_start(out=w1S.rearrange("p (cc h) -> p cc h", cc=CC), in_=wv).then_inc(s_w1.h, 16),
                                 [(s_w1, 16)])
                    P.wait("tensor", s_w2, w2_t, "w2")
                    P.wait("tensor", s_hp, s_hp.n, "eohp")
                    for tl in range(4):
                        ti = tc * 4 + tl
                        for ch in range(2):
                            bank = psf[4 + (tl * 2 + ch) % 2]
                            if s_eomm.n >= 2:
                                P.wait("tensor", s_eoev, s_eomm.n - 1, "eobank")
                            for hh2 in range(HH):
                                last = (hh2 == HH - 1)
                                t_op(lambda t, b=bank, l=hp[tc % 2][:, hh2 * 512 + tl * 128: hh2 * 512 + (tl + 1) * 128],
                                     r=w2S[:, hh2 * D + ch * 512: hh2 * D + (ch + 1) * 512],
                                     st=(hh2 == 0), sp=last, li=last:
                                     (t.matmul(b[:], l, r, start=st, stop=sp).then_inc(s_eomm.h, 1) if li
                                      else t.matmul(b[:], l, r, start=st, stop=sp)),
                                     [(s_eomm, 1)] if last else ())
                            P.wait("vector", s_eomm, s_eomm.n, "eoev")
                            P.wait("vector", s_gcb, gcb_t, "comb")
                            xsl = xg[:, ti * D + ch * 512: ti * D + (ch + 1) * 512]
                            v_op(lambda v, o=xsl, b=bank, cw=comb[:, ti * E + e: ti * E + e + 1]:
                                 v.scalar_tensor_tensor(out=o, in0=b[:], scalar=cw, in1=o, op0=ALU.mult, op1=ALU.add).then_inc(s_eoev.h, 1),
                                 [(s_eoev, 1)])
                if e < E - 1 or prefetch_next_pass:
                    ep = (e + 1) % E
                    P.wait("gpsimd", s_eomm, s_eomm.n, "w2pre")
                    w2v = w2_in[ep].rearrange("(hh p) d -> p hh d", p=128)
                    g_op(lambda g, wv=w2v: g.dma_start(out=w2S.rearrange("p (hh d) -> p hh d", hh=HH), in_=wv).then_inc(s_w2.h, 16),
                         [(s_w2, 16)])

            P.wait("sync", s_eoev, s_eoev.n, "store")
            sy_op(lambda s: s.dma_start(out=dst_v[t0:t0 + NTG].rearrange("tt p d -> p tt d"),
                                        in_=xg.rearrange("p (tt d) -> p tt d", d=D)).then_inc(s_spill.h, 16),
                  [(s_spill, 16)])

        ffn1_dst = y_v if upto == "ffn1" else x1_v
        ffn_pass(0, x_v, ffn1_dst, 0, first_weights=True, prefetch_next_pass=(n_groups > 1))
        if n_groups > 1:
            ffn_pass(1, x_v, ffn1_dst, NTG, first_weights=False, prefetch_next_pass=False)

        if upto != "ffn1":
            spill_ffn1 = s_spill.n
            P.wait("tensor", s_hp, s_hp.n, "attn entry hp")
            P.wait("tensor", s_eoev, s_eoev.n, "attn entry eoev")
            P.wait("gpsimd", s_hmm, s_hmm.n, "wa gate")
            P.wait("gpsimd", s_eomm, s_eomm.n, "wa gate2")
            wa_v = wa_in.rearrange("(cc p) d -> p cc d", p=128)
            g_op(lambda g: g.dma_start(out=waS.rearrange("p (cc d) -> p cc d", cc=CC), in_=wa_v).then_inc(s_wa.h, 16), [(s_wa, 16)])
            wa_t = s_wa.n

            if smoke == "x1rt":
                for i in range(NQ):
                    P.wait("sync", s_spill, spill_ffn1 + (16 * (i - 1) if i >= 2 else 0), "rt")
                    sy_op(lambda s, i_=i: s.dma_start(out=xs_f[:, (i_ % 2) * 1024:(i_ % 2 + 1) * 1024],
                                                      in_=x1_v[i_]).then_inc(s_xs.h, 16), [(s_xs, 16)])
                    P.wait("sync", s_xs, s_xs.n, "rt2")
                    sy_op(lambda s, i_=i: s.dma_start(out=y_v[i_], in_=xs_f[:, (i_ % 2) * 1024:(i_ % 2 + 1) * 1024]).then_inc(s_spill.h, 16),
                          [(s_spill, 16)])
                P.wait("sync", s_spill, s_spill.n, "final")
                P.verify()
                emit_all()
                for cm in sem_ctxs:
                    cm.__exit__(None, None, None)
                return nc

            sq_base = s_sq.n
            xs_tgt = {}
            def get_tile2(i):
                P.wait("sync", s_spill, spill_ffn1, "x1ready")
                if i >= 2:
                    P.wait("sync", s_sq, sq_base + 2 * (i - 1), "xsreuse")
                sy_op(lambda s, i_=i: s.dma_start(out=xs_f[:, (i_ % 2) * 1024:(i_ % 2 + 1) * 1024],
                                                  in_=x1_v[i_]).then_inc(s_xs.h, 16), [(s_xs, 16)])
                xs_tgt[i] = s_xs.n
                return xs_f[:, (i % 2) * 1024:(i % 2 + 1) * 1024], [(s_xs, xs_tgt[i])]
            nm2, rs2, st2_t = ln_stats(NTS, get_tile2)
            if smoke == "stats2":
                P.wait("vector", s_stats, st2_t, "smoke stats2")
                v_op(lambda v: v.tensor_copy(out=x2st_f[:, 0:64], in_=stat[:, 0:64]).then_inc(s_oev.h, 1), [(s_oev, 1)])
                P.wait("sync", s_oev, s_oev.n, "stg")
                sy_op(lambda s: s.dma_start(out=y_v[0][:, 0:64], in_=x2st_f[:, 0:64]).then_inc(s_spill.h, 16), [(s_spill, 16)])
                P.wait("sync", s_spill, s_spill.n, "final")
                P.verify()
                emit_all()
                for cm in sem_ctxs:
                    cm.__exit__(None, None, None)
                return nc

            xn2_base = s_xn.n
            for tc in range(NK):
                for tl in range(4):
                    i = tc * 4 + tl
                    if i >= 2:
                        P.wait("sync", s_xn, xn2_base + i - 1, "xs2reuse")
                    sy_op(lambda s, i_=i: s.dma_start(out=xs_f[:, (i_ % 2) * 1024:(i_ % 2 + 1) * 1024],
                                                      in_=x1_v[i_]).then_inc(s_xs.h, 16), [(s_xs, 16)])
                    P.wait("vector", s_xs, s_xs.n, "xn2")
                    P.wait("vector", s_stats, st2_t, "xn2params")
                    if tc >= 2 and tl == 0:
                        P.wait("vector", s_fmm, s_fmm.n - CC, "xnT2free")
                    xb = xn2_b[:, (i % 2) * 1024:(i % 2 + 1) * 1024]
                    v_op(lambda v, o=xb, t=xs_f[:, (i % 2) * 1024:(i % 2 + 1) * 1024],
                         a=nm2[:, i:i + 1], r=rs2[:, i:i + 1]:
                         v.tensor_scalar(out=o, in0=t, scalar1=a, scalar2=r, op0=ALU.add, op1=ALU.mult).then_inc(s_xn.h, 1),
                         [(s_xn, 1)])
                    P.op("vector", lambda v: (v.nop(), v.nop(), v.nop()), ())
                    if smoke == "xn2st" and i < NQ:
                        if i >= 2:
                            P.wait("vector", s_spill, s_spill.n - 16, "stg free")
                        v_op(lambda v, xb_=xb, t=i: v.tensor_copy(out=x2st_f[:, (t % 2) * 1024:(t % 2 + 1) * 1024],
                                                                  in_=xb_).then_inc(s_oev.h, 1), [(s_oev, 1)])
                        P.wait("sync", s_oev, s_oev.n, "stg")
                        sy_op(lambda s, t=i: s.dma_start(out=y_v[t], in_=x2st_f[:, (t % 2) * 1024:(t % 2 + 1) * 1024]).then_inc(s_spill.h, 16),
                              [(s_spill, 16)])
                    xn_transpose(i, xb, 512, tl * 128, xnT2[:, (tc % 2) * 4096:(tc % 2 + 1) * 4096])
                P.wait("tensor", s_wa, wa_t, "wa")
                P.wait("tensor", s_xnt, s_xnt.n, "ft xnt")
                for dd in range(CC):
                    bank = psf[dd % 4]
                    if s_fmm.n >= 4:
                        P.wait("tensor", s_fev, s_fmm.n - 3, "ftbank")
                    for cc2 in range(CC):
                        last = (cc2 == CC - 1)
                        t_op(lambda t, b=bank, l=waS[:, cc2 * D + dd * 128: cc2 * D + (dd + 1) * 128],
                             r=xnT2[:, (tc % 2) * 4096 + cc2 * 512: (tc % 2) * 4096 + (cc2 + 1) * 512],
                             st=(cc2 == 0), sp=last, li=last:
                             (t.matmul(b[:], l, r, start=st, stop=sp).then_inc(s_fmm.h, 1) if li
                              else t.matmul(b[:], l, r, start=st, stop=sp)),
                             [(s_fmm, 1)] if last else ())
                    P.wait("vector", s_fmm, s_fmm.n, "fev")
                    v_op(lambda v, o=fT[:, dd * S + tc * 512: dd * S + (tc + 1) * 512], b=bank, bc=baS[:, dd:dd + 1]:
                         v.tensor_scalar_add(out=o, in0=b[:], scalar1=bc).then_inc(s_fev.h, 1), [(s_fev, 1)])
            fT_done = s_fev.n

            if smoke == "ftok_pre":
                pass

            for tt in range(NTS):
                for dd in range(CC):
                    P.wait("tensor", s_fev, fT_done, "ftok")
                    P.wait("tensor", s_xnt, s_tr.n, "pstr f")
                    t_op(lambda t, o=pstr[:, 0:128], i_=fT[:, dd * S + tt * 128: dd * S + (tt + 1) * 128]:
                         t.transpose(o, i_, idnS[:]).then_inc(s_tr.h, 1), [(s_tr, 1)])
                    P.wait("vector", s_tr, s_tr.n, "ftokev")
                    v_op(lambda v, o=f_tok[:, tt * D + dd * 128: tt * D + (dd + 1) * 128], i_=pstr[:, 0:128]:
                         v.tensor_copy(out=o, in_=i_).then_inc(s_xnt.h, 1), [(s_xnt, 1)])
            ftok_t = s_xnt.n

            if smoke == "ft":
                for dd in range(CC):
                    for hh in range(2):
                        ti = dd * 2 + hh
                        P.wait("vector", s_fev, fT_done, "smoke ft")
                        if ti >= 2:
                            P.wait("vector", s_spill, s_spill.n - 16, "stage free")
                        v_op(lambda v, d=dd, h=hh, t=ti: v.tensor_copy(out=x2st_f[:, (t % 2) * 1024:(t % 2 + 1) * 1024],
                                                                       in_=fT[:, d * S + h * 1024: d * S + (h + 1) * 1024]).then_inc(s_oev.h, 1), [(s_oev, 1)])
                        P.wait("sync", s_oev, s_oev.n, "smoke st")
                        sy_op(lambda s, t=ti: s.dma_start(out=y_v[t], in_=x2st_f[:, (t % 2) * 1024:(t % 2 + 1) * 1024]).then_inc(s_spill.h, 16),
                              [(s_spill, 16)])
                P.wait("sync", s_spill, s_spill.n, "final")
                P.verify()
                emit_all()
                for cm in sem_ctxs:
                    cm.__exit__(None, None, None)
                return nc

            if smoke == "ftok":
                for tt in range(NQ):
                    P.wait("vector", s_xnt, ftok_t, "smoke ftok")
                    if tt >= 2:
                        P.wait("vector", s_spill, s_spill.n - 16, "stage free")
                    v_op(lambda v, t=tt: v.tensor_copy(out=x2st_f[:, (t % 2) * 1024:(t % 2 + 1) * 1024],
                                                       in_=f_tok[:, t * D:(t + 1) * D]).then_inc(s_oev.h, 1), [(s_oev, 1)])
                    P.wait("sync", s_oev, s_oev.n, "smoke st")
                    sy_op(lambda s, t=tt: s.dma_start(out=y_v[t], in_=x2st_f[:, (t % 2) * 1024:(t % 2 + 1) * 1024]).then_inc(s_spill.h, 16),
                          [(s_spill, 16)])
                P.wait("sync", s_spill, s_spill.n, "final")
                P.verify()
                emit_all()
                for cm in sem_ctxs:
                    cm.__exit__(None, None, None)
                return nc

            bCC = bandS[:, 0:128]; bCL = bandS[:, 128:256]; bCR = bandS[:, 256:384]
            bEA = bandS[:, 384:512]; bEB = bandS[:, 512:640]
            spill_attn = s_spill.n
            sev_base = s_sev.n
            aT_tr_done = {}
            for qi in range(NQ):
                P.wait("sync", s_oev, max(0, s_oev.n - 2), "x1obuf")
                sy_op(lambda s, q=qi: s.dma_start(out=x1o_f[:, (q % 2) * 1024:(q % 2 + 1) * 1024],
                                                  in_=x1_v[q]).then_inc(s_x1o.h, 16), [(s_x1o, 16)])
                x1o_t = s_x1o.n
                for kc in range(NK):
                    bank = psf[kc % 4]
                    if s_smm.n >= 4:
                        P.wait("tensor", s_sev, s_smm.n - 3, "scbank")
                    P.wait("tensor", s_fev, fT_done, "scfT")
                    for dd in range(CC):
                        last = (dd == CC - 1)
                        t_op(lambda t, b=bank, l=fT[:, dd * S + qi * 128: dd * S + (qi + 1) * 128],
                             r=fT[:, dd * S + kc * 512: dd * S + (kc + 1) * 512],
                             st=(dd == 0), sp=last, li=last:
                             (t.matmul(b[:], l, r, start=st, stop=sp).then_inc(s_smm.h, 1) if li
                              else t.matmul(b[:], l, r, start=st, stop=sp)),
                             [(s_smm, 1)] if last else ())
                    P.wait("vector", s_smm, s_smm.n, "sev")
                    if qi >= 1:
                        P.wait("vector", s_exp, qi, "scoresfree")
                    v_op(lambda v, o=scoresS[:, kc * 512:(kc + 1) * 512], b=bank:
                         v.tensor_copy(out=o, in_=b[:]).then_inc(s_sev.h, 1), [(s_sev, 1)])
                P.wait("vector", s_const, CONST_T, "band")
                P.wait("vector", s_sev, s_sev.n, "evac barrier")
                def badd(tile, blk, last):
                    if last:
                        v_op(lambda v, b=tile, o=scoresS[:, blk * 128:(blk + 1) * 128]:
                             v.tensor_tensor(out=o, in0=o, in1=b, op=ALU.add).then_inc(s_vb.h, 1), [(s_vb, 1)])
                        P.wait("vector", s_vb, s_vb.n, "badd barrier")
                    else:
                        v_op(lambda v, b=tile, o=scoresS[:, blk * 128:(blk + 1) * 128]:
                             v.tensor_tensor(out=o, in0=o, in1=b, op=ALU.add))
                blks = [(bCC, qi)]
                if qi >= 1: blks.append((bCL, qi - 1))
                if qi <= 14: blks.append((bCR, qi + 1))
                if qi == 15: blks.append((bEA, 16))
                if qi == 0: blks.append((bEB, 31))
                for bi, (tile_, blk_) in enumerate(blks):
                    badd(tile_, blk_, bi == len(blks) - 1)
                if smoke == "sc0" and qi == 0:
                    for h4 in range(4):
                        if h4 >= 2:
                            P.wait("vector", s_spill, s_spill.n - 16, "stg free")
                        v_op(lambda v, h=h4: v.tensor_copy(out=x2st_f[:, (h % 2) * 1024:(h % 2 + 1) * 1024],
                                                           in_=scoresS[:, h * 1024:(h + 1) * 1024]).then_inc(s_oev.h, 1), [(s_oev, 1)])
                        P.wait("sync", s_oev, s_oev.n, "stg")
                        sy_op(lambda s, h=h4: s.dma_start(out=y_v[h], in_=x2st_f[:, (h % 2) * 1024:(h % 2 + 1) * 1024]).then_inc(s_spill.h, 16),
                              [(s_spill, 16)])
                    P.wait("sync", s_spill, s_spill.n, "final")
                    P.verify()
                    emit_all()
                    for cm in sem_ctxs:
                        cm.__exit__(None, None, None)
                    return nc
                mcol = gm[:, 5:6]; nmcol = gm[:, 6:7]; rz = gm[:, 7:8]
                v_op(lambda v: v.tensor_reduce(out=mcol, in_=scoresS, axis=AX, op=ALU.max).then_inc(s_vb.h, 1), [(s_vb, 1)])
                P.wait("vector", s_vb, s_vb.n, "max barrier")
                v_op(lambda v: v.tensor_scalar_mul(out=nmcol, in0=mcol, scalar1=c(C_NEGONE)).then_inc(s_mx.h, 1), [(s_mx, 1)])
                P.wait("scalar", s_mx, s_mx.n, "exp")
                ab = a_b[qi % 2]
                if qi >= 2:
                    P.wait("scalar", s_tr, aT_tr_done[qi - 2], "areuse")
                sc_op(lambda s, o=ab, nm_=nmcol: s.activation(out=o, in_=scoresS, func=ACT.Exp, bias=nm_).then_inc(s_exp.h, 1),
                      [(s_exp, 1)])
                P.wait("vector", s_exp, s_exp.n, "zsum")
                v_op(lambda v, a_=ab: v.tensor_reduce(out=mcol, in_=a_, axis=AX, op=ALU.add).then_inc(s_vb.h, 1), [(s_vb, 1)])
                P.wait("vector", s_vb, s_vb.n, "zsum barrier")
                v_op(lambda v: v.reciprocal(out=rz, in_=mcol).then_inc(s_vb.h, 1), [(s_vb, 1)])
                P.wait("vector", s_vb, s_vb.n, "recip barrier")
                if smoke == "a0" and qi == 0:
                    P.wait("vector", s_exp, s_exp.n, "smoke a0")
                    for h4 in range(4):
                        if h4 >= 2:
                            P.wait("vector", s_spill, s_spill.n - 16, "stg free")
                        v_op(lambda v, h=h4: v.tensor_copy(out=x2st_f[:, (h % 2) * 1024:(h % 2 + 1) * 1024],
                                                           in_=ab[:, h * 1024:(h + 1) * 1024]).then_inc(s_oev.h, 1), [(s_oev, 1)])
                        P.wait("sync", s_oev, s_oev.n, "stg")
                        sy_op(lambda s, h=h4: s.dma_start(out=y_v[h], in_=x2st_f[:, (h % 2) * 1024:(h % 2 + 1) * 1024]).then_inc(s_spill.h, 16),
                              [(s_spill, 16)])
                    P.wait("sync", s_spill, s_spill.n, "final")
                    P.verify()
                    emit_all()
                    for cm in sem_ctxs:
                        cm.__exit__(None, None, None)
                    return nc
                if smoke == "rz" and qi == 0:
                    P.op("vector", lambda v: (v.nop(), v.nop(), v.nop()), ())
                    v_op(lambda v: v.tensor_copy(out=x2st_f[:, 0:8], in_=gm[:]).then_inc(s_oev.h, 1), [(s_oev, 1)])
                    P.wait("sync", s_oev, s_oev.n, "stg")
                    sy_op(lambda s: s.dma_start(out=y_v[0][:, 0:8], in_=x2st_f[:, 0:8]).then_inc(s_spill.h, 16), [(s_spill, 16)])
                    P.wait("sync", s_spill, s_spill.n, "final")
                    P.verify()
                    emit_all()
                    for cm in sem_ctxs:
                        cm.__exit__(None, None, None)
                    return nc
                atb = aT_b[qi % 2]
                if qi >= 2:
                    P.wait("vector", s_omm, 2 * (qi - 1), "aTreuse")
                for kc in range(NTS):
                    P.wait("tensor", s_exp, s_exp.n, "atr")
                    P.wait("tensor", s_xnt, s_tr.n, "pstr a")
                    t_op(lambda t, o=pstr[:, 0:128], i_=ab[:, kc * 128:(kc + 1) * 128]:
                         t.transpose(o, i_, idnS[:]).then_inc(s_tr.h, 1), [(s_tr, 1)])
                    P.wait("vector", s_tr, s_tr.n, "aTev")
                    v_op(lambda v, o=atb[:, kc * 128:(kc + 1) * 128], i_=pstr[:, 0:128]:
                         v.tensor_copy(out=o, in_=i_).then_inc(s_xnt.h, 1), [(s_xnt, 1)])
                aT_tr_done[qi] = s_tr.n
                P.wait("tensor", s_xnt, s_xnt.n, "aTdone")
                for ch in range(2):
                    bank = psf[4 + ch]
                    if s_omm.n >= 2:
                        P.wait("tensor", s_oev, s_omm.n - 1, "obank")
                    for kc in range(NTS):
                        last = (kc == NTS - 1)
                        t_op(lambda t, b=bank, l=atb[:, kc * 128:(kc + 1) * 128],
                             r=f_tok[:, kc * D + ch * 512: kc * D + (ch + 1) * 512],
                             st=(kc == 0), sp=last, li=last:
                             (t.matmul(b[:], l, r, start=st, stop=sp).then_inc(s_omm.h, 1) if li
                              else t.matmul(b[:], l, r, start=st, stop=sp)),
                             [(s_omm, 1)] if last else ())
                    P.wait("vector", s_omm, s_omm.n, "oev")
                    P.wait("vector", s_x1o, x1o_t, "x1o")
                    if qi >= 2 and ch == 0:
                        P.wait("vector", s_spill, spill_attn + 16 * (qi - 1), "x2stfree")
                    x2sl = x2st_f[:, (qi % 2) * 1024 + ch * 512: (qi % 2) * 1024 + (ch + 1) * 512]
                    x1sl = x1o_f[:, (qi % 2) * 1024 + ch * 512: (qi % 2) * 1024 + (ch + 1) * 512]
                    if smoke == "o1":
                        v_op(lambda v, o=x2sl, b=bank:
                             v.tensor_copy(out=o, in_=b[:]).then_inc(s_oev.h, 1), [(s_oev, 1)])
                    else:
                        v_op(lambda v, o=x2sl, b=bank, r1=x1sl:
                             v.scalar_tensor_tensor(out=o, in0=b[:], scalar=rz, in1=r1, op0=ALU.mult, op1=ALU.add).then_inc(s_oev.h, 1),
                             [(s_oev, 1)])
                P.wait("sync", s_oev, s_oev.n, "x2store")
                dst = y_v if upto == "attn" else x2_v
                sy_op(lambda s, q=qi, d=dst: s.dma_start(out=d[q], in_=x2st_f[:, (q % 2) * 1024:(q % 2 + 1) * 1024]).then_inc(s_spill.h, 16),
                      [(s_spill, 16)])
                if smoke in ("o0", "o1") and qi == 0:
                    P.wait("sync", s_spill, s_spill.n, "final")
                    P.verify()
                    emit_all()
                    for cm in sem_ctxs:
                        cm.__exit__(None, None, None)
                    return nc

            if upto == "full":
                ffn_pass(2, x2_v, y_v, 0, first_weights=True, prefetch_next_pass=False)

        P.wait("sync", s_spill, s_spill.n, "final")

        P.verify()
        emit_all()

        for cm in sem_ctxs:
            cm.__exit__(None, None, None)

    return nc


MAXD, SCALE = 10, 0.01

def cfun(d):
    d = np.abs(d).astype(np.float64)
    return (np.minimum(d, MAXD) - MAXD) * SCALE

def band_tiles(parity):
    p = np.arange(128)[:, None]
    j = np.arange(128)[None, :]
    CCt = cfun(p - j)
    CLt = cfun(p - j + 128)
    CRt = cfun(p - j - 128)
    EA = CRt if parity == 0 else np.zeros((128, 128))
    EB = CLt if parity == 1 else np.zeros((128, 128))
    return np.concatenate([CCt, CLt, CRt, EA, EB], axis=1).astype(np.float32)

def consts_tile():
    cst = np.zeros((128, 8), np.float32)
    cst[:, C_INV_D] = 1.0 / D
    cst[:, C_NEG_INV_D] = -1.0 / D
    cst[:, C_EPS] = 1e-5
    cst[:, C_HALF] = 0.5
    cst[:, C_NEGHALF] = -0.5
    cst[:, C_NEGBIG] = -1e30
    cst[:, C_ONE] = 1.0
    cst[:, C_NEGONE] = -1.0
    return cst

def make_in_map(core, x_full, W1, W2, Wg, Wa, ba):
    b = core // 2
    par = core % 2
    xb = x_full[b]
    if par == 0:
        xl = xb
    else:
        xl = np.concatenate([xb[G:S], xb[0:G]], axis=0)
    return {
        "x": np.ascontiguousarray(xl).astype(np.float32),
        "w1": W1, "w2": W2,
        "wg": np.ascontiguousarray(Wg.reshape(CC, 128, E).transpose(1, 0, 2).reshape(128, CC * E)),
        "wa": Wa,
        "ba": np.ascontiguousarray(ba.reshape(CC, 128).T),
        "cst": consts_tile(),
        "idn": np.eye(128, dtype=np.float32),
        "band": band_tiles(par),
    }


_NC_CACHE = {}

def _get_nc():
    if "nc" not in _NC_CACHE:
        _NC_CACHE["nc"] = build("full")
    return _NC_CACHE["nc"]


def kernel(L0_attn_x, L0_attn_W, L0_attn_b, L0_ffn_Wg, L0_ffn_W1, L0_ffn_W2):
    from concourse.bass_utils import run_bass_kernel_spmd
    x = np.ascontiguousarray(np.asarray(L0_attn_x), dtype=np.float32)
    W = np.ascontiguousarray(np.asarray(L0_attn_W), dtype=np.float32)
    ba = np.ascontiguousarray(np.asarray(L0_attn_b), dtype=np.float32)
    Wg = np.ascontiguousarray(np.asarray(L0_ffn_Wg), dtype=np.float32)
    W1 = np.ascontiguousarray(np.asarray(L0_ffn_W1), dtype=np.float32)
    W2 = np.ascontiguousarray(np.asarray(L0_ffn_W2), dtype=np.float32)

    nc = _get_nc()
    in_maps = [make_in_map(c, x, W1, W2, Wg, W, ba) for c in range(8)]
    res = run_bass_kernel_spmd(nc, in_maps, core_ids=list(range(8)))

    out = np.empty((4, 4096, 1024), np.float32)
    for c in range(8):
        b_ = c // 2
        par = c % 2
        out[b_, par * G:(par + 1) * G] = res.results[c]["y"]
    return out


# version 02
# speedup vs baseline: 1.0309x; 1.0309x over previous
import numpy as np
import concourse.bass as bass


class _Done(Exception):
    pass
import concourse.mybir as mybir

F32 = mybir.dt.float32
BF16 = mybir.dt.bfloat16
AX = mybir.AxisListType.X
ALU = mybir.AluOpType
ACT = mybir.ActivationFunctionType

D, H, E, S, G = 1024, 2048, 8, 4096, 2048
NTG = G // 128
NTS = S // 128
CC = D // 128
HH = H // 128
NK = S // 512
NQ = G // 128

C_INV_D, C_NEG_INV_D, C_EPS, C_HALF, C_NEGHALF, C_NEGBIG, C_ONE, C_NEGONE = range(8)

ENGINES = ("tensor", "vector", "scalar", "gpsimd", "sync")


class Sem:
    __slots__ = ("h", "name", "n")
    def __init__(self, h, name):
        self.h, self.name, self.n = h, name, 0


class Plan:
    def __init__(self):
        self.q = {e: [] for e in ENGINES}
        self.meta = {e: [] for e in ENGINES}

    def op(self, eng, fn, incs=()):
        self.q[eng].append(fn)
        self.meta[eng].append(("i", list(incs)))
        for s, k in incs:
            s.n += k

    def wait(self, eng, sem, val, tag=""):
        assert val <= sem.n, f"[{eng}] wait {sem.name}>={val} but only {sem.n} planned ({tag})"
        self.q[eng].append(lambda x, s=sem, v=val: x.wait_ge(s.h, v))
        self.meta[eng].append(("w", sem, val, tag))

    def verify(self):
        pos = {e: 0 for e in ENGINES}
        cur = {}
        progress = True
        while progress:
            progress = False
            for e in ENGINES:
                while pos[e] < len(self.meta[e]):
                    m = self.meta[e][pos[e]]
                    if m[0] == "w":
                        if cur.get(m[1].name, 0) < m[2]:
                            break
                    else:
                        for s, k in m[1]:
                            cur[s.name] = cur.get(s.name, 0) + k
                    pos[e] += 1
                    progress = True
        stuck = {e: (pos[e], len(self.meta[e])) for e in ENGINES if pos[e] < len(self.meta[e])}
        if stuck:
            det = {}
            for e in stuck:
                m = self.meta[e][pos[e]]
                det[e] = f"at {pos[e]}/{len(self.meta[e])}: waiting {m[1].name}>={m[2]} (cur {cur.get(m[1].name,0)}) tag={m[3]}"
            raise RuntimeError(f"DEADLOCK: {det}")


def build(upto="full", n_groups=2, n_experts=8, smoke=None):
    nc = bass.Bass("TRN2", target_bir_lowering=False, debug=False, detect_race_conditions=False)

    x_in = nc.dram_tensor("x", [S, D], F32, kind="ExternalInput").ap()
    w1_in = nc.dram_tensor("w1", [E, D, H], F32, kind="ExternalInput").ap()
    w2_in = nc.dram_tensor("w2", [E, H, D], F32, kind="ExternalInput").ap()
    wg_in = nc.dram_tensor("wg", [128, CC * E], F32, kind="ExternalInput").ap()
    wa_in = nc.dram_tensor("wa", [D, D], F32, kind="ExternalInput").ap()
    ba_in = nc.dram_tensor("ba", [128, CC], F32, kind="ExternalInput").ap()
    cst_in = nc.dram_tensor("cst", [128, 8], F32, kind="ExternalInput").ap()
    idn_in = nc.dram_tensor("idn", [128, 128], F32, kind="ExternalInput").ap()
    band_in = nc.dram_tensor("band", [128, 5 * 128], F32, kind="ExternalInput").ap()

    out_rows = S if upto == "ffn1" else G
    y_out = nc.dram_tensor("y", [out_rows, D], F32, kind="ExternalOutput").ap()
    x1buf = nc.dram_tensor("x1buf", [S, D], F32).ap()
    x2buf = nc.dram_tensor("x2buf", [G, D], F32).ap()

    x_v = x_in.rearrange("(tt p) d -> tt p d", p=128)
    x1_v = x1buf.rearrange("(tt p) d -> tt p d", p=128)
    x2_v = x2buf.rearrange("(tt p) d -> tt p d", p=128)
    y_v = y_out.rearrange("(tt p) d -> tt p d", p=128)

    P = Plan()

    from contextlib import ExitStack
    with ExitStack() as stack:
        def sb(name, shape, dt):
            return stack.enter_context(nc.sbuf_tensor(name, shape, dt))
        def pt(name, shape, dt):
            return stack.enter_context(nc.psum_tensor(name, shape, dt))
        big1 = sb("big1", [128, 32768], BF16)
        big2 = sb("big2", [128, 32768], BF16)
        bigw = sb("bigw", [128, 32768], BF16)
        sqt = sb("sqt", [128, 1024], F32)
        xnt_b = sb("xnt2", [128, 2 * 1024], BF16)
        t1b = sqt.ap().bitcast(BF16)[0:128, 0:1024]
        stat = sb("stat", [128, 64], F32)
        bab = sb("bab", [128, 128], F32)
        comb = sb("comb", [128, NTG * E], F32)
        gsc = sb("gsc", [128, 16], F32)
        gm = sb("gm", [128, 8], F32)
        cstS = sb("cstS", [128, 8], F32)
        idnS = sb("idnS", [128, 128], BF16)
        bandS = sb("bandS", [128, 5 * 128], F32)
        wgF = sb("wgF", [128, CC * E], F32)
        idnF = sb("idnF", [128, 128], F32)
        xtf = sb("xtf", [128, 512], F32)
        baS = sb("baS", [128, CC], F32)
        ps0 = pt("ps0", [128, 512], F32); ps1 = pt("ps1", [128, 512], F32)
        ps2 = pt("ps2", [128, 512], F32); ps3 = pt("ps3", [128, 512], F32)
        ps4 = pt("ps4", [128, 512], F32); ps5 = pt("ps5", [128, 512], F32)
        ps6 = pt("ps6", [128, 512], F32)
        pstr = pt("pstr", [128, 512], BF16)
        block = stack.enter_context(nc.Block())

        psf = [ps0, ps1, ps2, ps3, ps4, ps5, ps6]

        def emit_all():
            @block.tensor
            def _(t):
                for fn in P.q["tensor"]: fn(t)
            @block.vector
            def _(v):
                for fn in P.q["vector"]: fn(v)
            @block.scalar
            def _(s):
                for fn in P.q["scalar"]: fn(s)
            @block.gpsimd
            def _(g):
                for fn in P.q["gpsimd"]: fn(g)
            @block.sync
            def _(s):
                for fn in P.q["sync"]: fn(s)

        sems = {}
        sem_ctxs = []
        def sem(name):
            cm = nc.semaphore(name)
            h = cm.__enter__()
            sem_ctxs.append(cm)
            s = Sem(h, name)
            sems[name] = s
            return s

        s_const = sem("sconst"); s_const2 = sem("sconst2")
        s_ldx = sem("sldx")
        s_w1 = sem("sw1"); s_w2 = sem("sw2"); s_wa = sem("swa")
        s_sq = sem("ssq")
        s_stats = sem("sstats")
        s_xn = sem("sxn")
        s_tr = sem("str")
        s_xnt = sem("sxnt")
        s_gmm = sem("sgmm"); s_gps = sem("sgps"); s_gdd = sem("sgdd"); s_gx = sem("sgx"); s_gcb = sem("sgcb")
        s_xnf = sem("sxnf"); s_gtr = sem("sgtr"); s_gev = sem("sgev"); s_vb = sem("svb")
        s_hmm = sem("shmm"); s_t1 = sem("st1"); s_u = sem("su"); s_hp = sem("shp")
        s_eomm = sem("seomm"); s_eoev = sem("seoev")
        s_spill = sem("sspill")
        s_xs = sem("sxs")
        s_fmm = sem("sfmm"); s_fev = sem("sfev")
        s_smm = sem("ssmm"); s_sev = sem("ssev"); s_mx = sem("smx"); s_exp = sem("sexp")
        s_omm = sem("somm"); s_oev = sem("soev"); s_atr = sem("satr")
        s_x1o = sem("sx1o")

        def t_op(fn, incs=()): P.op("tensor", fn, incs)
        def vg_op(fn, incs=()):
            P.op("vector", lambda v: (v.nop(), v.nop(), v.nop()), ())
            P.op("vector", fn, incs)
        def v_op(fn, incs=()): P.op("vector", fn, incs)
        def sc_op(fn, incs=()): P.op("scalar", fn, incs)
        def g_op(fn, incs=()): P.op("gpsimd", fn, incs)
        def sy_op(fn, incs=()): P.op("sync", fn, incs)

        g_op(lambda g: g.dma_start(out=idnS[:], in_=idn_in).then_inc(s_const.h, 16), [(s_const, 16)])
        g_op(lambda g: g.dma_start(out=wgF[:], in_=wg_in).then_inc(s_const.h, 16), [(s_const, 16)])
        g_op(lambda g: g.dma_start(out=idnF[:], in_=idn_in).then_inc(s_const.h, 16), [(s_const, 16)])
        g_op(lambda g: g.dma_start(out=cstS[:], in_=cst_in).then_inc(s_const.h, 16), [(s_const, 16)])
        sy_op(lambda s: s.dma_start(out=bandS[:], in_=band_in).then_inc(s_const2.h, 16), [(s_const2, 16)])
        g_op(lambda g: g.dma_start(out=baS[:], in_=ba_in).then_inc(s_const.h, 16), [(s_const, 16)])
        CONST_T = s_const.n
        P.wait("vector", s_const2, s_const2.n, "const2 v")
        P.wait("vector", s_const, CONST_T, "const v")
        P.wait("scalar", s_const, CONST_T, "const s")
        P.wait("tensor", s_const, CONST_T, "const t")
        sc_op(lambda s: s.activation(out=gm[:, 0:1], in_=cstS[:, 0:1], func=ACT.Exp))

        def c(i):
            return cstS[:, i:i + 1]

        xg = big1.ap().bitcast(F32)
        xnT = big2[:, 0:CC * G]
        hp = [big2[:, 16384 + i * 8192: 16384 + (i + 1) * 8192] for i in range(2)]
        w1S = bigw[:, 0:CC * H]
        w2S = bigw[:, CC * H: CC * H + HH * D]
        fT = big1.ap()
        f_tok = big2.ap()
        aw = bigw.ap()
        waS = aw[0:128, 0:CC * D]
        xs_f = aw[0:128, CC * D: CC * D + 4096].bitcast(F32)
        xn2_b = aw[0:128, CC * D + 4096: CC * D + 4096 + 2048]
        xnT2 = aw[0:128, CC * D + 6144: CC * D + 6144 + 2 * 4096]
        scoresS = aw[0:128, 0:8192].bitcast(F32)
        a_b = [aw[0:128, 8192 + i * 4096: 8192 + (i + 1) * 4096] for i in range(2)]
        aT_b = [aw[0:128, 16384 + i * 4096: 16384 + (i + 1) * 4096] for i in range(2)]
        x1o_f = aw[0:128, 24576:24576 + 4096].bitcast(F32)
        x2st_f = aw[0:128, 28672:28672 + 4096].bitcast(F32)

        def ln_stats(ntile, get_tile):
            base_sq = s_sq.n
            sums = bab[:, 96:96 + ntile] if False else stat[:, 0:ntile]
            ssq = bab[:, 96:96 + ntile]
            for i in range(ntile):
                tile, waits = get_tile(i)
                for sx, vv in waits:
                    P.wait("scalar", sx, vv, "lnst")
                P.wait("scalar", s_sq, s_sq.n, "sq waw")
                sc_op(lambda s, t=tile, o=sums[:, i:i + 1]: s.activation(out=sqt[:], in_=t, func=ACT.Copy, accum_out=o).then_inc(s_sq.h, 1), [(s_sq, 1)])
                P.wait("scalar", s_sq, s_sq.n, "sq waw2")
                sc_op(lambda s, t=tile, o=ssq[:, i:i + 1]: s.activation(out=sqt[:], in_=t, func=ACT.Square, accum_out=o).then_inc(s_sq.h, 1), [(s_sq, 1)])
            nm = stat[:, 0:ntile]
            rs = stat[:, 32:32 + ntile]
            a_ = bab[:, 0:ntile]; z_ = bab[:, 32:32 + ntile]; t_ = bab[:, 64:64 + ntile]
            P.wait("vector", s_sq, base_sq + 2 * ntile, "params")
            vg_op(lambda v: v.tensor_scalar_mul(out=t_, in0=sums, scalar1=c(C_INV_D)))
            vg_op(lambda v: v.tensor_scalar_mul(out=a_, in0=ssq, scalar1=c(C_INV_D)))
            vg_op(lambda v: v.tensor_tensor(out=z_, in0=t_, in1=t_, op=ALU.mult))
            vg_op(lambda v: v.tensor_tensor(out=a_, in0=a_, in1=z_, op=ALU.subtract))
            vg_op(lambda v: v.tensor_scalar_add(out=a_, in0=a_, scalar1=c(C_EPS)))
            vg_op(lambda v: v.tensor_scalar_mul(out=nm, in0=t_, scalar1=c(C_NEGONE)))
            vg_op(lambda v: v.tensor_scalar(out=z_, in0=a_, scalar1=c(C_HALF), scalar2=c(C_HALF), op0=ALU.mult, op1=ALU.add))
            for _ in range(13):
                vg_op(lambda v: v.reciprocal(out=t_, in_=z_))
                vg_op(lambda v: v.tensor_tensor(out=t_, in0=a_, in1=t_, op=ALU.mult))
                vg_op(lambda v: v.tensor_tensor(out=t_, in0=t_, in1=z_, op=ALU.add))
                vg_op(lambda v: v.tensor_scalar_mul(out=z_, in0=t_, scalar1=c(C_HALF)))
            vg_op(lambda v: v.reciprocal(out=rs, in_=z_).then_inc(s_stats.h, 1), [(s_stats, 1)])
            return nm, rs, s_stats.n

        def xn_transpose(i, xb, dst_cc_stride, dst_col, src_tile):
            for half in range(2):
                P.wait("tensor", s_xn, s_xn.n, "trxn")
                P.wait("tensor", s_xnt, s_tr.n, "pstrfree")
                for k in range(4):
                    cc2 = half * 4 + k
                    last = (k == 3)
                    t_op(lambda t, o=pstr[:, k * 128:(k + 1) * 128], i_=xb[:, cc2 * 128:(cc2 + 1) * 128], li=last:
                         (t.transpose(o, i_, idnS[:]).then_inc(s_tr.h, 1) if li
                          else t.transpose(o, i_, idnS[:])),
                         [(s_tr, 1)] if last else ())
                P.wait("vector", s_tr, s_tr.n, "evtr")
                for k in range(4):
                    cc2 = half * 4 + k
                    last = (k == 3)
                    v_op(lambda v, o=src_tile[:, cc2 * dst_cc_stride + dst_col: cc2 * dst_cc_stride + dst_col + 128],
                         i_=pstr[:, k * 128:(k + 1) * 128], li=last:
                         (v.tensor_copy(out=o, in_=i_).then_inc(s_xnt.h, 1) if li
                          else v.tensor_copy(out=o, in_=i_)),
                         [(s_xnt, 1)] if last else ())

        def emit_gating_tail(i):
                sc0 = gsc[:, 0:E]; s2 = gsc[:, E:2 * E]
                m1 = gm[:, 0:1]; m2 = gm[:, 1:2]; p1 = gm[:, 2:3]; p2 = gm[:, 3:4]; dd_ = gm[:, 4:5]
                P.wait("vector", s_gmm, s_gmm.n, "gev")
                vg_op(lambda v: v.tensor_copy(out=sc0, in_=ps6[:, 0:E]).then_inc(s_gps.h, 1), [(s_gps, 1)])
                vg_op(lambda v: v.tensor_reduce(out=m1, in_=sc0, axis=AX, op=ALU.max))
                vg_op(lambda v: v.tensor_scalar(out=s2, in0=sc0, scalar1=m1, scalar2=None, op0=ALU.is_ge))
                vg_op(lambda v: v.scalar_tensor_tensor(out=s2, in0=s2, scalar=c(C_NEGBIG), in1=sc0, op0=ALU.mult, op1=ALU.add))
                vg_op(lambda v: v.tensor_reduce(out=m2, in_=s2, axis=AX, op=ALU.max))
                vg_op(lambda v: v.tensor_tensor(out=dd_, in0=m2, in1=m1, op=ALU.subtract).then_inc(s_gdd.h, 1), [(s_gdd, 1)])
                P.wait("scalar", s_gdd, s_gdd.n, "gexp")
                sc_op(lambda s: s.activation(out=p2, in_=dd_, func=ACT.Exp).then_inc(s_gx.h, 1), [(s_gx, 1)])
                P.wait("vector", s_gx, s_gx.n, "gp1")
                vg_op(lambda v: v.tensor_scalar_add(out=p2, in0=p2, scalar1=c(C_ONE)))
                vg_op(lambda v: v.reciprocal(out=p1, in_=p2))
                vg_op(lambda v: v.tensor_scalar(out=p2, in0=p1, scalar1=c(C_NEGHALF), scalar2=c(C_HALF), op0=ALU.mult, op1=ALU.add))
                vg_op(lambda v: v.tensor_scalar_mul(out=p1, in0=p1, scalar1=c(C_HALF)))
                cmb = comb[:, i * E:(i + 1) * E]
                vg_op(lambda v, cm=cmb: v.tensor_scalar(out=cm, in0=s2, scalar1=m2, scalar2=None, op0=ALU.is_ge))
                vg_op(lambda v, cm=cmb: v.tensor_scalar_mul(out=cm, in0=cm, scalar1=p2))
                vg_op(lambda v: v.tensor_scalar(out=s2, in0=sc0, scalar1=m1, scalar2=None, op0=ALU.is_ge))
                vg_op(lambda v, cm=cmb: v.scalar_tensor_tensor(out=cm, in0=s2, scalar=p1, in1=cm, op0=ALU.mult, op1=ALU.add).then_inc(s_gcb.h, 1),
                     [(s_gcb, 1)])

        def emit_xn_and_gating(i, tile_f32, nm, rs, st_t, dst, dst_stride, dst_col, gate):
            P.wait("vector", s_stats, st_t, "xnparams")
            if gate and s_gtr.n >= 2:
                P.wait("vector", s_gtr, s_gtr.n - 1, "sqt free")
            v_op(lambda v, t=tile_f32, a=nm[:, i:i + 1], r=rs[:, i:i + 1]:
                 v.tensor_scalar(out=sqt[:], in0=t, scalar1=a, scalar2=r, op0=ALU.add, op1=ALU.mult).then_inc(s_xnf.h, 1),
                 [(s_xnf, 1)])
            P.op("vector", lambda v: (v.nop(), v.nop(), v.nop()), ())
            xb = xnt_b[:, (i % 2) * 1024:(i % 2 + 1) * 1024]
            if i >= 2:
                P.wait("vector", s_tr, s_tr.n - 2, "xnbuf")
            v_op(lambda v, o=xb: v.tensor_copy(out=o, in_=sqt[:]).then_inc(s_xn.h, 1), [(s_xn, 1)])
            xn_transpose(i, xb, dst_stride, dst_col, dst)
            if not gate:
                return
            for half in range(2):
                P.wait("tensor", s_xnf, s_xnf.n, "gtr src")
                if s_gtr.n >= 1:
                    P.wait("tensor", s_gev, s_gtr.n, "ps5 free")
                for k in range(4):
                    cc2 = half * 4 + k
                    last = (k == 3)
                    t_op(lambda t, o=ps5[:, k * 128:(k + 1) * 128], i_=sqt[:, cc2 * 128:(cc2 + 1) * 128], li=last:
                         (t.transpose(o, i_, idnF[:]).then_inc(s_gtr.h, 1) if li
                          else t.transpose(o, i_, idnF[:])),
                         [(s_gtr, 1)] if last else ())
                P.wait("vector", s_gtr, s_gtr.n, "gev")
                for k in range(4):
                    cc2 = half * 4 + k
                    last = (k == 3)
                    v_op(lambda v, o=xtf[:, (cc2 % 4) * 128:(cc2 % 4 + 1) * 128], i_=ps5[:, k * 128:(k + 1) * 128], li=last:
                         (v.tensor_copy(out=o, in_=i_).then_inc(s_gev.h, 1) if li
                          else v.tensor_copy(out=o, in_=i_)),
                         [(s_gev, 1)] if last else ())
                P.wait("tensor", s_gev, s_gev.n, "gmm src")
                if half == 0 and s_gmm.n >= 1:
                    P.wait("tensor", s_gps, s_gmm.n, "gpsfree")
                for k in range(4):
                    cc2 = half * 4 + k
                    last = (half == 1 and k == 3)
                    t_op(lambda t, l=xtf[:, (cc2 % 4) * 128:(cc2 % 4 + 1) * 128],
                         r=wgF[:, cc2 * E:(cc2 + 1) * E], st=(cc2 == 0), sp=last, li=last:
                         (t.matmul(ps6[:, 0:E], l, r, start=st, stop=sp).then_inc(s_gmm.h, 1) if li
                          else t.matmul(ps6[:, 0:E], l, r, start=st, stop=sp)),
                         [(s_gmm, 1)] if last else ())
            emit_gating_tail(i)

        GTC = [0]

        def ffn_pass(pass_idx, src_v, dst_v, t0, first_weights, prefetch_next_pass):
            P.wait("tensor", s_sev, s_sev.n, "ffn entry sev")
            P.wait("tensor", s_oev, s_oev.n, "ffn entry oev")
            P.wait("sync", s_spill, s_spill.n, "xgfree")
            sy_op(lambda s: s.dma_start(out=xg.rearrange("p (tt d) -> p tt d", d=D),
                                        in_=src_v[t0:t0 + NTG].rearrange("tt p d -> p tt d")).then_inc(s_ldx.h, 16), [(s_ldx, 16)])
            ldx_t = s_ldx.n

            if first_weights:
                P.wait("gpsimd", s_spill, s_spill.n, "w after spills")
                g_op(lambda g, wv=w1_in[0].rearrange("(cc p) h -> p cc h", p=128):
                     g.dma_start(out=w1S.rearrange("p (cc h) -> p cc h", cc=CC), in_=wv).then_inc(s_w1.h, 16), [(s_w1, 16)])
                g_op(lambda g, wv=w2_in[0].rearrange("(hh p) d -> p hh d", p=128):
                     g.dma_start(out=w2S.rearrange("p (hh d) -> p hh d", hh=HH), in_=wv).then_inc(s_w2.h, 16), [(s_w2, 16)])

            def get_tile(i):
                return xg[:, i * D:(i + 1) * D], [(s_ldx, ldx_t)]
            if smoke == "copy":
                P.wait("sync", s_ldx, ldx_t, "smoke copy")
                sy_op(lambda s: s.dma_start(out=dst_v[t0:t0 + NTG].rearrange("tt p d -> p tt d"),
                                            in_=xg.rearrange("p (tt d) -> p tt d", d=D)).then_inc(s_spill.h, 16),
                      [(s_spill, 16)])
                return
            nm, rs, st_t = ln_stats(NTG, get_tile)
            if smoke == "stats":
                P.wait("sync", s_stats, st_t, "smoke stats")
                sy_op(lambda s: s.dma_start(out=dst_v[t0:t0 + NTG].rearrange("tt p d -> p tt d"),
                                            in_=xg.rearrange("p (tt d) -> p tt d", d=D)).then_inc(s_spill.h, 16),
                      [(s_spill, 16)])
                return

            for i in range(NTG):
                emit_xn_and_gating(i, xg[:, i * D:(i + 1) * D], nm, rs, st_t, xnT, G, i * 128,
                                   gate=(smoke != "xn"))
            xnt_t = s_xnt.n
            if smoke == "xn":
                P.wait("sync", s_xnt, xnt_t, "smoke xn")
                sy_op(lambda s: s.dma_start(out=dst_v[t0:t0 + NTG].rearrange("tt p d -> p tt d"),
                                            in_=xg.rearrange("p (tt d) -> p tt d", d=D)).then_inc(s_spill.h, 16),
                      [(s_spill, 16)])
                return
            gcb_t = s_gcb.n
            if smoke == "gate":
                P.wait("sync", s_gcb, gcb_t, "smoke gate")
                sy_op(lambda s: s.dma_start(out=dst_v[t0:t0 + NTG].rearrange("tt p d -> p tt d"),
                                            in_=xg.rearrange("p (tt d) -> p tt d", d=D)).then_inc(s_spill.h, 16),
                      [(s_spill, 16)])
                return

            for e in range(n_experts):
                w1_t = 16 * (pass_idx * E + e + 1)
                w2_t = 16 * (pass_idx * E + e + 1)
                for tc in range(4):
                    gtc = GTC[0]; GTC[0] += 1
                    P.wait("tensor", s_w1, w1_t, "w1")
                    P.wait("tensor", s_xnt, xnt_t, "h xnt")
                    first_stt = True
                    for hf in range(HH):
                        bank = psf[hf % 4]
                        if s_hmm.n >= 4:
                            P.wait("tensor", s_hp, s_hmm.n - 3, "hbank")
                        for cc2 in range(CC):
                            last = (cc2 == CC - 1)
                            t_op(lambda t, b=bank, l=w1S[:, cc2 * H + hf * 128: cc2 * H + (hf + 1) * 128],
                                 r=xnT[:, cc2 * G + tc * 512: cc2 * G + (tc + 1) * 512],
                                 st=(cc2 == 0), sp=last, li=last:
                                 (t.matmul(b[:], l, r, start=st, stop=sp).then_inc(s_hmm.h, 1) if li
                                  else t.matmul(b[:], l, r, start=st, stop=sp)),
                                 [(s_hmm, 1)] if last else ())
                        P.wait("scalar", s_hmm, s_hmm.n, "tanh")
                        if s_t1.n >= 2:
                            P.wait("scalar", s_hp, s_t1.n - 1, "t1reuse")
                        t1 = t1b[:, (hf % 2) * 512:(hf % 2 + 1) * 512]
                        sc_op(lambda s, o=t1, b=bank: s.activation(out=o, in_=b[:], func=ACT.Tanh, scale=0.5).then_inc(s_t1.h, 1),
                              [(s_t1, 1)])
                        P.wait("vector", s_t1, s_t1.n, "u")
                        uu = t1
                        v_op(lambda v, o=uu: v.tensor_scalar_add(out=o, in0=o, scalar1=c(C_ONE)).then_inc(s_u.h, 1),
                             [(s_u, 1)])
                        P.op("vector", lambda v: (v.nop(), v.nop(), v.nop()), ())
                        if first_stt and gtc >= 2:
                            P.wait("vector", s_eomm, 8 * (gtc - 1), "hpfree")
                        first_stt = False
                        hpd = hp[tc % 2][:, hf * 512:(hf + 1) * 512]
                        v_op(lambda v, o=hpd, b=bank, u_=uu:
                             v.scalar_tensor_tensor(out=o, in0=b[:], scalar=c(C_HALF), in1=u_, op0=ALU.mult, op1=ALU.mult).then_inc(s_hp.h, 1),
                             [(s_hp, 1)])
                    if tc == 3:
                        if e < E - 1 or prefetch_next_pass:
                            ep = (e + 1) % E
                            P.wait("gpsimd", s_hmm, s_hmm.n, "w1pre")
                            w1v = w1_in[ep].rearrange("(cc p) h -> p cc h", p=128)
                            g_op(lambda g, wv=w1v: g.dma_start(out=w1S.rearrange("p (cc h) -> p cc h", cc=CC), in_=wv).then_inc(s_w1.h, 16),
                                 [(s_w1, 16)])
                    P.wait("tensor", s_w2, w2_t, "w2")
                    hp_base = s_hp.n - HH
                    for tl in range(4):
                        ti = tc * 4 + tl
                        for ch in range(2):
                            bank = psf[4 + (tl * 2 + ch) % 2]
                            if s_eomm.n >= 2:
                                P.wait("tensor", s_eoev, s_eomm.n - 1, "eobank")
                            for hh2 in range(HH):
                                last = (hh2 == HH - 1)
                                if tl == 0 and ch == 0:
                                    P.wait("tensor", s_hp, hp_base + hh2 + 1, "eohp")
                                t_op(lambda t, b=bank, l=hp[tc % 2][:, hh2 * 512 + tl * 128: hh2 * 512 + (tl + 1) * 128],
                                     r=w2S[:, hh2 * D + ch * 512: hh2 * D + (ch + 1) * 512],
                                     st=(hh2 == 0), sp=last, li=last:
                                     (t.matmul(b[:], l, r, start=st, stop=sp).then_inc(s_eomm.h, 1) if li
                                      else t.matmul(b[:], l, r, start=st, stop=sp)),
                                     [(s_eomm, 1)] if last else ())
                            P.wait("vector", s_eomm, s_eomm.n, "eoev")
                            P.wait("vector", s_gcb, gcb_t, "comb")
                            xsl = xg[:, ti * D + ch * 512: ti * D + (ch + 1) * 512]
                            v_op(lambda v, o=xsl, b=bank, cw=comb[:, ti * E + e: ti * E + e + 1]:
                                 v.scalar_tensor_tensor(out=o, in0=b[:], scalar=cw, in1=o, op0=ALU.mult, op1=ALU.add).then_inc(s_eoev.h, 1),
                                 [(s_eoev, 1)])
                if e < E - 1 or prefetch_next_pass:
                    ep = (e + 1) % E
                    P.wait("gpsimd", s_eomm, s_eomm.n, "w2pre")
                    w2v = w2_in[ep].rearrange("(hh p) d -> p hh d", p=128)
                    g_op(lambda g, wv=w2v: g.dma_start(out=w2S.rearrange("p (hh d) -> p hh d", hh=HH), in_=wv).then_inc(s_w2.h, 16),
                         [(s_w2, 16)])

            P.wait("sync", s_eoev, s_eoev.n, "store")
            sy_op(lambda s: s.dma_start(out=dst_v[t0:t0 + NTG].rearrange("tt p d -> p tt d"),
                                        in_=xg.rearrange("p (tt d) -> p tt d", d=D)).then_inc(s_spill.h, 16),
                  [(s_spill, 16)])

        ffn1_dst = y_v if upto == "ffn1" else x1_v
        ffn_pass(0, x_v, ffn1_dst, 0, first_weights=True, prefetch_next_pass=(n_groups > 1))
        if n_groups > 1:
            ffn_pass(1, x_v, ffn1_dst, NTG, first_weights=False, prefetch_next_pass=False)

        if upto != "ffn1":
            spill_ffn1 = s_spill.n
            P.wait("tensor", s_hp, s_hp.n, "attn entry hp")
            P.wait("tensor", s_eoev, s_eoev.n, "attn entry eoev")
            P.wait("gpsimd", s_hmm, s_hmm.n, "wa gate")
            P.wait("gpsimd", s_eomm, s_eomm.n, "wa gate2")
            wa_v = wa_in.rearrange("(cc p) d -> p cc d", p=128)
            g_op(lambda g: g.dma_start(out=waS.rearrange("p (cc d) -> p cc d", cc=CC), in_=wa_v).then_inc(s_wa.h, 16), [(s_wa, 16)])
            wa_t = s_wa.n

            if smoke == "x1rt":
                for i in range(NQ):
                    P.wait("sync", s_spill, spill_ffn1 + (16 * (i - 1) if i >= 2 else 0), "rt")
                    sy_op(lambda s, i_=i: s.dma_start(out=xs_f[:, (i_ % 2) * 1024:(i_ % 2 + 1) * 1024],
                                                      in_=x1_v[i_]).then_inc(s_xs.h, 16), [(s_xs, 16)])
                    P.wait("sync", s_xs, s_xs.n, "rt2")
                    sy_op(lambda s, i_=i: s.dma_start(out=y_v[i_], in_=xs_f[:, (i_ % 2) * 1024:(i_ % 2 + 1) * 1024]).then_inc(s_spill.h, 16),
                          [(s_spill, 16)])
                P.wait("sync", s_spill, s_spill.n, "final")
                P.verify()
                emit_all()
                for cm in sem_ctxs:
                    cm.__exit__(None, None, None)
                return nc

            sq_base = s_sq.n
            xs_tgt = {}
            def get_tile2(i):
                P.wait("sync", s_spill, spill_ffn1, "x1ready")
                if i >= 2:
                    P.wait("sync", s_sq, sq_base + 2 * (i - 1), "xsreuse")
                sy_op(lambda s, i_=i: s.dma_start(out=xs_f[:, (i_ % 2) * 1024:(i_ % 2 + 1) * 1024],
                                                  in_=x1_v[i_]).then_inc(s_xs.h, 16), [(s_xs, 16)])
                xs_tgt[i] = s_xs.n
                return xs_f[:, (i % 2) * 1024:(i % 2 + 1) * 1024], [(s_xs, xs_tgt[i])]
            nm2, rs2, st2_t = ln_stats(NTS, get_tile2)
            if smoke == "stats2":
                P.wait("vector", s_stats, st2_t, "smoke stats2")
                v_op(lambda v: v.tensor_copy(out=x2st_f[:, 0:64], in_=stat[:, 0:64]).then_inc(s_oev.h, 1), [(s_oev, 1)])
                P.wait("sync", s_oev, s_oev.n, "stg")
                sy_op(lambda s: s.dma_start(out=y_v[0][:, 0:64], in_=x2st_f[:, 0:64]).then_inc(s_spill.h, 16), [(s_spill, 16)])
                P.wait("sync", s_spill, s_spill.n, "final")
                P.verify()
                emit_all()
                for cm in sem_ctxs:
                    cm.__exit__(None, None, None)
                return nc

            xn2_base = s_xn.n
            for tc in range(NK):
                for tl in range(4):
                    i = tc * 4 + tl
                    if i >= 2:
                        P.wait("sync", s_xn, xn2_base + i - 1, "xs2reuse")
                    sy_op(lambda s, i_=i: s.dma_start(out=xs_f[:, (i_ % 2) * 1024:(i_ % 2 + 1) * 1024],
                                                      in_=x1_v[i_]).then_inc(s_xs.h, 16), [(s_xs, 16)])
                    P.wait("vector", s_xs, s_xs.n, "xn2")
                    P.wait("vector", s_stats, st2_t, "xn2params")
                    if tc >= 2 and tl == 0:
                        P.wait("vector", s_fmm, s_fmm.n - CC, "xnT2free")
                    xb = xn2_b[:, (i % 2) * 1024:(i % 2 + 1) * 1024]
                    v_op(lambda v, o=xb, t=xs_f[:, (i % 2) * 1024:(i % 2 + 1) * 1024],
                         a=nm2[:, i:i + 1], r=rs2[:, i:i + 1]:
                         v.tensor_scalar(out=o, in0=t, scalar1=a, scalar2=r, op0=ALU.add, op1=ALU.mult).then_inc(s_xn.h, 1),
                         [(s_xn, 1)])
                    P.op("vector", lambda v: (v.nop(), v.nop(), v.nop()), ())
                    if smoke == "xn2st" and i < NQ:
                        if i >= 2:
                            P.wait("vector", s_spill, s_spill.n - 16, "stg free")
                        v_op(lambda v, xb_=xb, t=i: v.tensor_copy(out=x2st_f[:, (t % 2) * 1024:(t % 2 + 1) * 1024],
                                                                  in_=xb_).then_inc(s_oev.h, 1), [(s_oev, 1)])
                        P.wait("sync", s_oev, s_oev.n, "stg")
                        sy_op(lambda s, t=i: s.dma_start(out=y_v[t], in_=x2st_f[:, (t % 2) * 1024:(t % 2 + 1) * 1024]).then_inc(s_spill.h, 16),
                              [(s_spill, 16)])
                    xn_transpose(i, xb, 512, tl * 128, xnT2[:, (tc % 2) * 4096:(tc % 2 + 1) * 4096])
                P.wait("tensor", s_wa, wa_t, "wa")
                P.wait("tensor", s_xnt, s_xnt.n, "ft xnt")
                for dd in range(CC):
                    bank = psf[dd % 4]
                    if s_fmm.n >= 4:
                        P.wait("tensor", s_fev, s_fmm.n - 3, "ftbank")
                    for cc2 in range(CC):
                        last = (cc2 == CC - 1)
                        t_op(lambda t, b=bank, l=waS[:, cc2 * D + dd * 128: cc2 * D + (dd + 1) * 128],
                             r=xnT2[:, (tc % 2) * 4096 + cc2 * 512: (tc % 2) * 4096 + (cc2 + 1) * 512],
                             st=(cc2 == 0), sp=last, li=last:
                             (t.matmul(b[:], l, r, start=st, stop=sp).then_inc(s_fmm.h, 1) if li
                              else t.matmul(b[:], l, r, start=st, stop=sp)),
                             [(s_fmm, 1)] if last else ())
                    P.wait("vector", s_fmm, s_fmm.n, "fev")
                    v_op(lambda v, o=fT[:, dd * S + tc * 512: dd * S + (tc + 1) * 512], b=bank, bc=baS[:, dd:dd + 1]:
                         v.tensor_scalar_add(out=o, in0=b[:], scalar1=bc).then_inc(s_fev.h, 1), [(s_fev, 1)])
            fT_done = s_fev.n

            if smoke == "ftok_pre":
                pass

            for tt in range(NTS):
                for dd in range(CC):
                    P.wait("tensor", s_fev, fT_done, "ftok")
                    P.wait("tensor", s_xnt, s_tr.n, "pstr f")
                    t_op(lambda t, o=pstr[:, 0:128], i_=fT[:, dd * S + tt * 128: dd * S + (tt + 1) * 128]:
                         t.transpose(o, i_, idnS[:]).then_inc(s_tr.h, 1), [(s_tr, 1)])
                    P.wait("vector", s_tr, s_tr.n, "ftokev")
                    v_op(lambda v, o=f_tok[:, tt * D + dd * 128: tt * D + (dd + 1) * 128], i_=pstr[:, 0:128]:
                         v.tensor_copy(out=o, in_=i_).then_inc(s_xnt.h, 1), [(s_xnt, 1)])
            ftok_t = s_xnt.n

            if smoke == "ft":
                for dd in range(CC):
                    for hh in range(2):
                        ti = dd * 2 + hh
                        P.wait("vector", s_fev, fT_done, "smoke ft")
                        if ti >= 2:
                            P.wait("vector", s_spill, s_spill.n - 16, "stage free")
                        v_op(lambda v, d=dd, h=hh, t=ti: v.tensor_copy(out=x2st_f[:, (t % 2) * 1024:(t % 2 + 1) * 1024],
                                                                       in_=fT[:, d * S + h * 1024: d * S + (h + 1) * 1024]).then_inc(s_oev.h, 1), [(s_oev, 1)])
                        P.wait("sync", s_oev, s_oev.n, "smoke st")
                        sy_op(lambda s, t=ti: s.dma_start(out=y_v[t], in_=x2st_f[:, (t % 2) * 1024:(t % 2 + 1) * 1024]).then_inc(s_spill.h, 16),
                              [(s_spill, 16)])
                P.wait("sync", s_spill, s_spill.n, "final")
                P.verify()
                emit_all()
                for cm in sem_ctxs:
                    cm.__exit__(None, None, None)
                return nc

            if smoke == "ftok":
                for tt in range(NQ):
                    P.wait("vector", s_xnt, ftok_t, "smoke ftok")
                    if tt >= 2:
                        P.wait("vector", s_spill, s_spill.n - 16, "stage free")
                    v_op(lambda v, t=tt: v.tensor_copy(out=x2st_f[:, (t % 2) * 1024:(t % 2 + 1) * 1024],
                                                       in_=f_tok[:, t * D:(t + 1) * D]).then_inc(s_oev.h, 1), [(s_oev, 1)])
                    P.wait("sync", s_oev, s_oev.n, "smoke st")
                    sy_op(lambda s, t=tt: s.dma_start(out=y_v[t], in_=x2st_f[:, (t % 2) * 1024:(t % 2 + 1) * 1024]).then_inc(s_spill.h, 16),
                          [(s_spill, 16)])
                P.wait("sync", s_spill, s_spill.n, "final")
                P.verify()
                emit_all()
                for cm in sem_ctxs:
                    cm.__exit__(None, None, None)
                return nc

            bCC = bandS[:, 0:128]; bCL = bandS[:, 128:256]; bCR = bandS[:, 256:384]
            bEA = bandS[:, 384:512]; bEB = bandS[:, 512:640]
            spill_attn = s_spill.n
            sev_base = s_sev.n
            aT_tr_done = {}
            for qi in range(NQ):
                P.wait("sync", s_oev, max(0, s_oev.n - 2), "x1obuf")
                sy_op(lambda s, q=qi: s.dma_start(out=x1o_f[:, (q % 2) * 1024:(q % 2 + 1) * 1024],
                                                  in_=x1_v[q]).then_inc(s_x1o.h, 16), [(s_x1o, 16)])
                x1o_t = s_x1o.n
                for kc in range(NK):
                    bank = psf[kc % 6]
                    if s_smm.n >= 6:
                        P.wait("tensor", s_sev, s_smm.n - 5, "scbank")
                    P.wait("tensor", s_fev, fT_done, "scfT")
                    for dd in range(CC):
                        last = (dd == CC - 1)
                        t_op(lambda t, b=bank, l=fT[:, dd * S + qi * 128: dd * S + (qi + 1) * 128],
                             r=fT[:, dd * S + kc * 512: dd * S + (kc + 1) * 512],
                             st=(dd == 0), sp=last, li=last:
                             (t.matmul(b[:], l, r, start=st, stop=sp).then_inc(s_smm.h, 1) if li
                              else t.matmul(b[:], l, r, start=st, stop=sp)),
                             [(s_smm, 1)] if last else ())
                    P.wait("vector", s_smm, s_smm.n, "sev")
                    if qi >= 1:
                        P.wait("vector", s_exp, qi, "scoresfree")
                    v_op(lambda v, o=scoresS[:, kc * 512:(kc + 1) * 512], b=bank:
                         v.tensor_copy(out=o, in_=b[:]).then_inc(s_sev.h, 1), [(s_sev, 1)])
                P.wait("vector", s_const, CONST_T, "band")
                P.wait("vector", s_sev, s_sev.n, "evac barrier")
                def badd(tile, blk, last):
                    if last:
                        v_op(lambda v, b=tile, o=scoresS[:, blk * 128:(blk + 1) * 128]:
                             v.tensor_tensor(out=o, in0=o, in1=b, op=ALU.add).then_inc(s_vb.h, 1), [(s_vb, 1)])
                        P.wait("vector", s_vb, s_vb.n, "badd barrier")
                    else:
                        v_op(lambda v, b=tile, o=scoresS[:, blk * 128:(blk + 1) * 128]:
                             v.tensor_tensor(out=o, in0=o, in1=b, op=ALU.add))
                blks = [(bCC, qi)]
                if qi >= 1: blks.append((bCL, qi - 1))
                if qi <= 14: blks.append((bCR, qi + 1))
                if qi == 15: blks.append((bEA, 16))
                if qi == 0: blks.append((bEB, 31))
                for bi, (tile_, blk_) in enumerate(blks):
                    badd(tile_, blk_, bi == len(blks) - 1)
                if smoke == "sc0" and qi == 0:
                    for h4 in range(4):
                        if h4 >= 2:
                            P.wait("vector", s_spill, s_spill.n - 16, "stg free")
                        v_op(lambda v, h=h4: v.tensor_copy(out=x2st_f[:, (h % 2) * 1024:(h % 2 + 1) * 1024],
                                                           in_=scoresS[:, h * 1024:(h + 1) * 1024]).then_inc(s_oev.h, 1), [(s_oev, 1)])
                        P.wait("sync", s_oev, s_oev.n, "stg")
                        sy_op(lambda s, h=h4: s.dma_start(out=y_v[h], in_=x2st_f[:, (h % 2) * 1024:(h % 2 + 1) * 1024]).then_inc(s_spill.h, 16),
                              [(s_spill, 16)])
                    P.wait("sync", s_spill, s_spill.n, "final")
                    P.verify()
                    emit_all()
                    for cm in sem_ctxs:
                        cm.__exit__(None, None, None)
                    return nc
                mcol = gm[:, 5:6]; nmcol = gm[:, 6:7]; rz = gm[:, 7:8]
                v_op(lambda v: v.tensor_reduce(out=mcol, in_=scoresS, axis=AX, op=ALU.max).then_inc(s_vb.h, 1), [(s_vb, 1)])
                P.wait("vector", s_vb, s_vb.n, "max barrier")
                v_op(lambda v: v.tensor_scalar_mul(out=nmcol, in0=mcol, scalar1=c(C_NEGONE)).then_inc(s_mx.h, 1), [(s_mx, 1)])
                P.wait("scalar", s_mx, s_mx.n, "exp")
                ab = a_b[qi % 2]
                if qi >= 2:
                    P.wait("scalar", s_tr, aT_tr_done[qi - 2], "areuse")
                sc_op(lambda s, o=ab, nm_=nmcol: s.activation(out=o, in_=scoresS, func=ACT.Exp, bias=nm_).then_inc(s_exp.h, 1),
                      [(s_exp, 1)])
                P.wait("vector", s_exp, s_exp.n, "zsum")
                v_op(lambda v, a_=ab: v.tensor_reduce(out=mcol, in_=a_, axis=AX, op=ALU.add).then_inc(s_vb.h, 1), [(s_vb, 1)])
                P.wait("vector", s_vb, s_vb.n, "zsum barrier")
                v_op(lambda v: v.reciprocal(out=rz, in_=mcol).then_inc(s_vb.h, 1), [(s_vb, 1)])
                P.wait("vector", s_vb, s_vb.n, "recip barrier")
                if smoke == "a0" and qi == 0:
                    P.wait("vector", s_exp, s_exp.n, "smoke a0")
                    for h4 in range(4):
                        if h4 >= 2:
                            P.wait("vector", s_spill, s_spill.n - 16, "stg free")
                        v_op(lambda v, h=h4: v.tensor_copy(out=x2st_f[:, (h % 2) * 1024:(h % 2 + 1) * 1024],
                                                           in_=ab[:, h * 1024:(h + 1) * 1024]).then_inc(s_oev.h, 1), [(s_oev, 1)])
                        P.wait("sync", s_oev, s_oev.n, "stg")
                        sy_op(lambda s, h=h4: s.dma_start(out=y_v[h], in_=x2st_f[:, (h % 2) * 1024:(h % 2 + 1) * 1024]).then_inc(s_spill.h, 16),
                              [(s_spill, 16)])
                    P.wait("sync", s_spill, s_spill.n, "final")
                    P.verify()
                    emit_all()
                    for cm in sem_ctxs:
                        cm.__exit__(None, None, None)
                    return nc
                if smoke == "rz" and qi == 0:
                    P.op("vector", lambda v: (v.nop(), v.nop(), v.nop()), ())
                    v_op(lambda v: v.tensor_copy(out=x2st_f[:, 0:8], in_=gm[:]).then_inc(s_oev.h, 1), [(s_oev, 1)])
                    P.wait("sync", s_oev, s_oev.n, "stg")
                    sy_op(lambda s: s.dma_start(out=y_v[0][:, 0:8], in_=x2st_f[:, 0:8]).then_inc(s_spill.h, 16), [(s_spill, 16)])
                    P.wait("sync", s_spill, s_spill.n, "final")
                    P.verify()
                    emit_all()
                    for cm in sem_ctxs:
                        cm.__exit__(None, None, None)
                    return nc
                atb = aT_b[qi % 2]
                if qi >= 2:
                    P.wait("vector", s_omm, 2 * (qi - 1), "aTreuse")
                for kc in range(NTS):
                    P.wait("tensor", s_exp, s_exp.n, "atr")
                    P.wait("tensor", s_xnt, s_tr.n, "pstr a")
                    t_op(lambda t, o=pstr[:, 0:128], i_=ab[:, kc * 128:(kc + 1) * 128]:
                         t.transpose(o, i_, idnS[:]).then_inc(s_tr.h, 1), [(s_tr, 1)])
                    P.wait("vector", s_tr, s_tr.n, "aTev")
                    v_op(lambda v, o=atb[:, kc * 128:(kc + 1) * 128], i_=pstr[:, 0:128]:
                         v.tensor_copy(out=o, in_=i_).then_inc(s_xnt.h, 1), [(s_xnt, 1)])
                aT_tr_done[qi] = s_tr.n
                P.wait("tensor", s_xnt, s_xnt.n, "aTdone")
                for ch in range(2):
                    bank = psf[6]
                    if s_omm.n >= 1:
                        P.wait("tensor", s_oev, s_omm.n, "obank")
                    for kc in range(NTS):
                        last = (kc == NTS - 1)
                        t_op(lambda t, b=bank, l=atb[:, kc * 128:(kc + 1) * 128],
                             r=f_tok[:, kc * D + ch * 512: kc * D + (ch + 1) * 512],
                             st=(kc == 0), sp=last, li=last:
                             (t.matmul(b[:], l, r, start=st, stop=sp).then_inc(s_omm.h, 1) if li
                              else t.matmul(b[:], l, r, start=st, stop=sp)),
                             [(s_omm, 1)] if last else ())
                    P.wait("vector", s_omm, s_omm.n, "oev")
                    P.wait("vector", s_x1o, x1o_t, "x1o")
                    if qi >= 2 and ch == 0:
                        P.wait("vector", s_spill, spill_attn + 16 * (qi - 1), "x2stfree")
                    x2sl = x2st_f[:, (qi % 2) * 1024 + ch * 512: (qi % 2) * 1024 + (ch + 1) * 512]
                    x1sl = x1o_f[:, (qi % 2) * 1024 + ch * 512: (qi % 2) * 1024 + (ch + 1) * 512]
                    if smoke == "o1":
                        v_op(lambda v, o=x2sl, b=bank:
                             v.tensor_copy(out=o, in_=b[:]).then_inc(s_oev.h, 1), [(s_oev, 1)])
                    else:
                        v_op(lambda v, o=x2sl, b=bank, r1=x1sl:
                             v.scalar_tensor_tensor(out=o, in0=b[:], scalar=rz, in1=r1, op0=ALU.mult, op1=ALU.add).then_inc(s_oev.h, 1),
                             [(s_oev, 1)])
                P.wait("sync", s_oev, s_oev.n, "x2store")
                dst = y_v if upto == "attn" else x2_v
                sy_op(lambda s, q=qi, d=dst: s.dma_start(out=d[q], in_=x2st_f[:, (q % 2) * 1024:(q % 2 + 1) * 1024]).then_inc(s_spill.h, 16),
                      [(s_spill, 16)])
                if smoke in ("o0", "o1") and qi == 0:
                    P.wait("sync", s_spill, s_spill.n, "final")
                    P.verify()
                    emit_all()
                    for cm in sem_ctxs:
                        cm.__exit__(None, None, None)
                    return nc

            if upto == "full":
                ffn_pass(2, x2_v, y_v, 0, first_weights=True, prefetch_next_pass=False)

        P.wait("sync", s_spill, s_spill.n, "final")

        P.verify()
        emit_all()

        for cm in sem_ctxs:
            cm.__exit__(None, None, None)

    return nc


MAXD, SCALE = 10, 0.01

def cfun(d):
    d = np.abs(d).astype(np.float64)
    return (np.minimum(d, MAXD) - MAXD) * SCALE

def band_tiles(parity):
    p = np.arange(128)[:, None]
    j = np.arange(128)[None, :]
    CCt = cfun(p - j)
    CLt = cfun(p - j + 128)
    CRt = cfun(p - j - 128)
    EA = CRt if parity == 0 else np.zeros((128, 128))
    EB = CLt if parity == 1 else np.zeros((128, 128))
    return np.concatenate([CCt, CLt, CRt, EA, EB], axis=1).astype(np.float32)

def consts_tile():
    cst = np.zeros((128, 8), np.float32)
    cst[:, C_INV_D] = 1.0 / D
    cst[:, C_NEG_INV_D] = -1.0 / D
    cst[:, C_EPS] = 1e-5
    cst[:, C_HALF] = 0.5
    cst[:, C_NEGHALF] = -0.5
    cst[:, C_NEGBIG] = -1e30
    cst[:, C_ONE] = 1.0
    cst[:, C_NEGONE] = -1.0
    return cst

def make_in_map(core, x_full, W1, W2, Wg, Wa, ba):
    b = core // 2
    par = core % 2
    xb = x_full[b]
    if par == 0:
        xl = xb
    else:
        xl = np.concatenate([xb[G:S], xb[0:G]], axis=0)
    return {
        "x": np.ascontiguousarray(xl).astype(np.float32),
        "w1": W1, "w2": W2,
        "wg": np.ascontiguousarray(Wg.reshape(CC, 128, E).transpose(1, 0, 2).reshape(128, CC * E)),
        "wa": Wa,
        "ba": np.ascontiguousarray(ba.reshape(CC, 128).T),
        "cst": consts_tile(),
        "idn": np.eye(128, dtype=np.float32),
        "band": band_tiles(par),
    }


_NC_CACHE = {}

def _get_nc():
    if "nc" not in _NC_CACHE:
        _NC_CACHE["nc"] = build("full")
    return _NC_CACHE["nc"]


def kernel(L0_attn_x, L0_attn_W, L0_attn_b, L0_ffn_Wg, L0_ffn_W1, L0_ffn_W2):
    from concourse.bass_utils import run_bass_kernel_spmd
    x = np.ascontiguousarray(np.asarray(L0_attn_x), dtype=np.float32)
    W = np.ascontiguousarray(np.asarray(L0_attn_W), dtype=np.float32)
    ba = np.ascontiguousarray(np.asarray(L0_attn_b), dtype=np.float32)
    Wg = np.ascontiguousarray(np.asarray(L0_ffn_Wg), dtype=np.float32)
    W1 = np.ascontiguousarray(np.asarray(L0_ffn_W1), dtype=np.float32)
    W2 = np.ascontiguousarray(np.asarray(L0_ffn_W2), dtype=np.float32)

    nc = _get_nc()
    in_maps = [make_in_map(c, x, W1, W2, Wg, W, ba) for c in range(8)]
    res = run_bass_kernel_spmd(nc, in_maps, core_ids=list(range(8)))

    out = np.empty((4, 4096, 1024), np.float32)
    for c in range(8):
        b_ = c // 2
        par = c % 2
        out[b_, par * G:(par + 1) * G] = res.results[c]["y"]
    return out
